# version 01 (final)
# baseline (speedup 1.0000x reference)
import os
import numpy as np
import ml_dtypes
from contextlib import ExitStack
import concourse.bass as bass
import concourse.mybir as mybir
from concourse.bass_types import DynSlice
from concourse.bass_utils import run_bass_kernel_spmd

F32 = mybir.dt.float32
BF16 = mybir.dt.bfloat16
I32 = mybir.dt.int32
AF = mybir.ActivationFunctionType
OP = mybir.AluOpType
AX = mybir.AxisListType
bfd = ml_dtypes.bfloat16

NC = 8
TOWN = 1024
HALO = 3
TT = TOWN + HALO
D = 2048
KD = 16
PROJ = 4640
DSSM = 2048
CONVD = 2560
NH, PH, NG, GS = 32, 64, 8, 32
NKV, HQ, DH = 4, 16, 128
WIN = 512
F_MOE, NE = 1024, 16
NCH = 8
CH = 128
EPS_LN = 1e-5
EPS_GN = 1e-5
NEG = -30000.0

PHASES = int(os.environ.get("KDEV_PHASES", "99"))


class Sem:
    def __init__(self, nc, es, name):
        self.h = es.enter_context(nc.semaphore(name))
        self.n = 0
    def inc(self, instr, k=1):
        instr.then_inc(self.h, k)
        self.n += k
        return self.n


class K:
    pass


def ceil_div(a, b):
    return (a + b - 1) // b


def prep_inputs(inp):
    x = np.asarray(inp["L0_attn_a_hidden_states"])[0]
    W_in = np.ascontiguousarray(inp["L0_attn_a_W_in"])
    b_in = np.asarray(inp["L0_attn_a_b_in"])
    conv_w = np.asarray(inp["L0_attn_a_conv_w"])
    conv_b = np.asarray(inp["L0_attn_a_conv_b"])
    A_log = np.asarray(inp["L0_attn_a_A_log"])
    D_param = np.asarray(inp["L0_attn_a_D_param"])
    dt_bias = np.asarray(inp["L0_attn_a_dt_bias"])
    norm_w = np.asarray(inp["L0_attn_a_norm_w"])
    W_out = np.ascontiguousarray(inp["L0_attn_a_W_out"])
    b_out = np.asarray(inp["L0_attn_a_b_out"])
    mup = np.asarray(inp["L0_attn_a_mup_vector"])
    Wq = np.asarray(inp["L0_attn_b_Wq"])
    Wk = np.asarray(inp["L0_attn_b_Wk"])
    Wv = np.asarray(inp["L0_attn_b_Wv"])
    Wo = np.asarray(inp["L0_attn_b_Wo"])
    w_gate = np.ascontiguousarray(inp["L0_ffn_w_gate"])
    w_g = np.ascontiguousarray(inp["L0_ffn_w_g"])
    w_u = np.ascontiguousarray(inp["L0_ffn_w_u"])
    w_d = np.ascontiguousarray(inp["L0_ffn_w_d"])
    ebias = np.asarray(inp["L0_ffn_expert_bias"])

    xs = []
    for i in range(NC):
        lo = i * TOWN - HALO
        if lo < 0:
            sl = np.concatenate([np.zeros((HALO, D), np.float32), x[0:i*TOWN+TOWN]], axis=0)
        else:
            sl = x[lo:(i+1)*TOWN]
        xs.append(np.ascontiguousarray(sl.T.reshape(KD, 128, TT)))

    wq_t = np.ascontiguousarray(Wq.T)
    wk_t = np.ascontiguousarray(Wk.T)
    wv_t = np.ascontiguousarray(Wv.T)
    wo_t = np.ascontiguousarray(Wo.T)

    def cols(v, ntile):
        out = np.zeros((128, ntile), np.float32)
        v = v.reshape(-1)
        for j in range(ntile):
            seg = v[j*128:(j+1)*128]
            out[:len(seg), j] = seg
        return out

    bin_cols = cols(b_in, 37)
    mup_cols = cols(mup, 37)
    convw_t = np.ascontiguousarray(conv_w.reshape(20, 128, 4).transpose(1, 0, 2))
    convb_cols = cols(conv_b, 20)
    bout_cols = cols(b_out, 16)

    inv = 1.0 / (10000.0 ** (np.arange(0, DH, 2, dtype=np.float64) / DH))
    def rope_tabs(pos):
        ang = pos[None, :] * inv[:, None]
        c = np.cos(ang); s = np.sin(ang)
        cos128 = np.concatenate([c, c], axis=0)
        sin128 = np.concatenate([-s, s], axis=0)
        return cos128.astype(np.float32), sin128.astype(np.float32)
    scale = 1.0 / np.sqrt(np.float64(DH))
    ropes = []
    for i in range(NC):
        pos = np.arange(i*TOWN, (i+1)*TOWN, dtype=np.float64)
        cq, sq = rope_tabs(pos)
        ropes.append((np.ascontiguousarray((cq*scale).astype(np.float32)),
                      np.ascontiguousarray((sq*scale).astype(np.float32)),
                      cq.copy(), sq.copy()))

    def mask_tiles(core):
        M = np.full((2, 8, 128, 512), NEG, np.float32)
        for qc in range(2):
            for kt in range(8):
                m = 128*kt + np.arange(128)[:, None]
                l = np.arange(512)[None, :]
                valid = (l >= m - 512) & (l < m)
                if core == 0:
                    valid &= (qc*512 + m - 512) >= 0
                M[qc, kt][valid] = 0.0
        return M

    sel_h = np.zeros((32, 32*128), np.float32)
    hsel = np.zeros((32, KD*128), np.float32)
    for k in range(KD):
        hsel[2*k, k*128:k*128+64] = 1.0
        hsel[2*k+1, k*128+64:k*128+128] = 1.0
    esel = np.zeros((16, 16*128), np.float32)
    for e in range(16):
        esel[e, e*128:(e+1)*128] = 1.0

    ident = np.eye(128, dtype=np.float32)

    meta = {}
    in_maps = []
    for i in range(NC):
        cq, sq, ck, sk = ropes[i]
        m = {
            "x_fm": xs[i],
            "w_in": W_in,
            "bin_cols": bin_cols, "mup_cols": mup_cols,
            "bin_row": np.ascontiguousarray(b_in[None, :DSSM]),
            "mup_row": np.ascontiguousarray(mup[None, :DSSM]),
            "convw_t": convw_t, "convb_cols": convb_cols,
            "a_log": A_log.reshape(32, 1), "dt_bias": dt_bias.reshape(32, 1),
            "d_rep": np.ascontiguousarray(np.repeat(D_param, PH)[None, :]),
            "normw_row": norm_w[None, :], "bout_cols": bout_cols,
            "w_out": W_out,
            "wq_t": wq_t, "wk_t": wk_t, "wv_t": wv_t, "wo_t": wo_t,
            "cos_q": cq, "sin_q": sq, "cos_k": ck, "sin_k": sk,
            "att_mask": mask_tiles(i),
            "w_gate": w_gate,
            "w_g": w_g, "w_u": w_u, "w_d": w_d,
            "ebias_rep": np.ascontiguousarray(np.tile(ebias[None, :], (128, 1))),
            "hsel": hsel, "esel": esel, "ident": ident,
            "halo_mask": np.full((128, 1), 0.0 if i == 0 else 1.0, np.float32),
            "caus01": np.triu(np.ones((128, 128), np.float32)),
            "halo_src": np.array([[((i - 1) % NC) * 128]], np.int32),
            "rank_w": np.ascontiguousarray(
                np.array([[1.0 if k < i else 0.0 for k in range(8)]], np.float32)),
        }
        in_maps.append(m)
    return in_maps, meta


def build(debug=None):
    nc = bass.Bass("TRN2", target_bir_lowering=False)
    c = K()
    c.nc = nc
    es = ExitStack()
    c.es = es

    dr = {}
    def din(name, shape, dt=F32):
        dr[name] = nc.dram_tensor(name, list(shape), dt, kind="ExternalInput").ap()
    din("x_fm", [KD, 128, TT])
    din("w_in", [D, PROJ])
    din("bin_cols", [128, 37]); din("mup_cols", [128, 37])
    din("bin_row", [1, DSSM]); din("mup_row", [1, DSSM])
    din("convw_t", [128, 20, 4]); din("convb_cols", [128, 20])
    din("a_log", [32, 1]); din("dt_bias", [32, 1])
    din("d_rep", [1, DSSM]); din("normw_row", [1, DSSM]); din("bout_cols", [128, 16])
    din("w_out", [DSSM, D])
    din("wq_t", [D, D]); din("wk_t", [D, 512]); din("wv_t", [D, 512]); din("wo_t", [D, D])
    din("cos_q", [128, TOWN]); din("sin_q", [128, TOWN])
    din("cos_k", [128, TOWN]); din("sin_k", [128, TOWN])
    din("att_mask", [2, 8, 128, 512])
    din("w_gate", [D, NE])
    din("w_g", [NE, D, F_MOE]); din("w_u", [NE, D, F_MOE]); din("w_d", [NE, F_MOE, D])
    din("ebias_rep", [128, NE])
    din("hsel", [32, KD * 128]); din("esel", [16, 16 * 128]); din("ident", [128, 128])
    din("halo_mask", [128, 1]); din("halo_src", [1, 1], I32); din("rank_w", [1, 8])
    din("caus01", [128, 128])
    c.dr = dr
    out = nc.dram_tensor("out", [KD, 128, TOWN], F32, kind="ExternalOutput").ap()
    c.out = out

    c.z_dram = nc.dram_tensor("z_spill", [8, 128, D], BF16).ap()
    c.xtok_dram = nc.dram_tensor("xtok_spill", [NCH, 128, DSSM], BF16).ap()
    c.bg_dram = nc.dram_tensor("bg_spill", [2, 32, 8 * 512], BF16).ap()
    c.cg_dram = nc.dram_tensor("cg_spill", [2, 32, 8 * 512], BF16).ap()
    c.bd_dram = nc.dram_tensor("bd_spill", [NCH, 128, 1024], BF16).ap()
    c.stag_in = nc.dram_tensor("stag_in", [33, DSSM], F32)
    c.stag_out = nc.dram_tensor("stag_out", [NC * 33, DSSM], F32)
    c.kvag_in = nc.dram_tensor("kvag_in", [128, 4096], BF16)
    c.kvag_out = nc.dram_tensor("kvag_out", [NC * 128, 4096], BF16)

    dbg = {}
    c.dbg = dbg
    if debug == "ln1":
        dbg["ln"] = nc.dram_tensor("dbg_ln", [KD, 128, TT], F32, kind="ExternalOutput").ap()
    if debug == "inproj":
        dbg["xbc"] = nc.dram_tensor("dbg_xbc", [20, 128, 515], F32, kind="ExternalOutput").ap()
        dbg["dt"] = nc.dram_tensor("dbg_dt", [32, TT], F32, kind="ExternalOutput").ap()
        dbg["z"] = nc.dram_tensor("dbg_z", [8, 128, D], F32, kind="ExternalOutput").ap()
    if debug == "conv":
        dbg["xa"] = nc.dram_tensor("dbg_xa", [KD, 128, 512], F32, kind="ExternalOutput").ap()
        dbg["bg"] = nc.dram_tensor("dbg_bg", [32, 8 * 8 * CH], F32, kind="ExternalOutput").ap()
        dbg["cg"] = nc.dram_tensor("dbg_cg", [32, 8 * 8 * CH], F32, kind="ExternalOutput").ap()
    if debug == "mamba":
        dbg["y1"] = nc.dram_tensor("dbg_y1", [KD, 128, TOWN], F32, kind="ExternalOutput").ap()
        dbg["tapH"] = nc.dram_tensor("dbg_tapH", [4, 32, 2048], F32, kind="ExternalOutput").ap()

    sb = {}
    def sbuf(name, shape, dt=F32, alloc_name=None):
        sb[name] = es.enter_context(nc.sbuf_tensor(alloc_name or name, list(shape), dt))
        return sb[name]
    c.sb = sb
    sbuf("identb", [128, 128], BF16)
    sbuf("identf", [128, 128], F32)
    sbuf("ones1b", [1, 128], BF16)
    sbuf("ones1f", [1, 128], F32)
    sbuf("onescolb", [128, 1], BF16)
    sbuf("onescolf", [128, 1], F32)
    sbuf("bin_cols", [128, 37], F32, "sb_bin_cols"); sbuf("mup_cols", [128, 37], F32, "sb_mup_cols"); sbuf("binmup", [128, 37])
    sbuf("convw", [128, 80]); sbuf("convb", [128, 20])
    sbuf("alog", [32, 1]); sbuf("dtb", [32, 1]); sbuf("expA", [32, 1])
    sbuf("bout_cols", [128, 16], F32, "sb_bout_cols")
    sbuf("halo_mask", [128, 1], F32, "sb_halo_mask")
    sbuf("halo_src", [1, 1], I32, "sb_halo_src")
    sbuf("rank_w", [1, 8], F32, "sb_rank_w")
    sbuf("constv", [128, 4], F32)
    sbuf("caus01b", [128, 128], BF16)
    sbuf("monesf", [1, 128], F32)
    sbuf("zeros32", [32, 128], F32)
    sbuf("dtraw", [32, TT])

    sm = {}
    for s in ("ldw", "ldm", "lds", "pe", "dv", "act", "pool", "sts", "cc", "spl"):
        sm[s] = Sem(nc, es, s)
    c.sm = sm

    block = es.enter_context(nc.Block())
    c.block = block
    return c


def w(eng, sem):
    eng.wait_ge(sem.h, sem.n)


def gdma(g, sem, out, in_):
    g.wait_ge(sem.h, sem.n)
    ii = g.dma_start(out=out, in_=in_)
    sem.inc(ii, 16)
    return sem.n


def emit_phase0(c):
    nc, b, sb, dr, sm = c.nc, c.block, c.sb, c.dr, c.sm
    def sy_body(sy):
        for name, dst in (("bin_cols", "bin_cols"), ("mup_cols", "mup_cols"),
                          ("convb_cols", "convb"), ("a_log", "alog"), ("dt_bias", "dtb"),
                          ("bout_cols", "bout_cols"),
                          ("ident", "identf"), ("halo_src", "halo_src"), ("rank_w", "rank_w")):
            sy.dma_start(out=sb[dst][:], in_=dr[name][:]).then_inc(sm["lds"].h, 16)
            sm["lds"].n += 16
        sy.dma_start(out=sb["convw"].ap().rearrange("p (j k) -> p j k", k=4),
                     in_=dr["convw_t"][:]).then_inc(sm["lds"].h, 16)
        sm["lds"].n += 16
    b.sync(sy_body)
    lds_after = sm["lds"].n
    def g_body(g):
        g.dma_start(out=sb["identb"][:], in_=dr["ident"][:]).then_inc(sm["ldm"].h, 16)
        sm["ldm"].n += 16
        g.memset(sb["ones1b"][:], 1.0)
        g.memset(sb["ones1f"][:], 1.0)
        g.memset(sb["onescolf"][:], 1.0)
        g.memset(sb["constv"][:, 0:1], 1e-5)
        g.memset(sb["monesf"][:], -1.0)
        g.memset(sb["zeros32"][:], 0.0)
        g.memset(sb["constv"][:, 1:2], 1.0)
        mg = g.memset(sb["onescolb"][:], 1.0)
        sm["pool"].inc(mg, 1)
        g.dma_start(out=sb["halo_mask"][:], in_=dr["halo_mask"][:]).then_inc(sm["ldm"].h, 16)
        g.dma_start(out=sb["caus01b"][:], in_=dr["caus01"][:]).then_inc(sm["ldm"].h, 16)
        sm["ldm"].n += 32
    b.gpsimd(g_body)
    ldm_after = sm["ldm"].n
    def v_body(v):
        v.wait_ge(sm["ldm"].h, ldm_after)
        v.wait_ge(sm["lds"].h, lds_after)
        iv = v.tensor_mul(sb["binmup"][:], sb["bin_cols"][:], sb["mup_cols"][:])
        sm["dv"].inc(iv)
    b.vector(v_body)
    def s_body(s):
        s.wait_ge(sm["ldm"].h, ldm_after)
        s.wait_ge(sm["lds"].h, lds_after)
        ia = s.activation(out=sb["expA"][:], in_=sb["alog"][:], func=AF.Exp)
        sm["act"].inc(ia)
    b.scalar(s_body)


def emit_barrier(c, sems=None):
    b, sm = c.block, c.sm
    if sems is None:
        sems = [k for k in sm if sm[k].n > 0]
    for eng in ("tensor", "vector", "scalar", "gpsimd", "sync"):
        def body(e, sems=sems):
            for s in sems:
                e.wait_ge(sm[s].h, sm[s].n)
        getattr(b, eng)(body)


def emit_ln(c, col0, ncols, lnb, es2, deps=(), rows=None, src_dram=None, stride=None):
    nc, b, sb, sm = c.nc, c.block, c.sb, c.sm
    c.ln_calls = getattr(c, "ln_calls", 0) + 1
    xT = None if src_dram is not None else sb["xT"]
    if stride is None:
        stride = TT if src_dram is not None else TOWN
    if rows is None:
        row_mu = es2.enter_context(nc.sbuf_tensor(f"ln_mu{c.ln_calls}", [1, ncols], F32))
        row_rstd = es2.enter_context(nc.sbuf_tensor(f"ln_rs{c.ln_calls}", [1, ncols], F32))
        roff = col0
    else:
        row_mu, row_rstd, roff = rows
    scr = es2.enter_context(nc.sbuf_tensor(f"ln_scr{c.ln_calls}", [128, KD * 512], BF16))
    ps_sum = es2.enter_context(nc.psum_tensor(f"ps_lnsum{c.ln_calls}", [1, 512], F32))
    ps_sq = es2.enter_context(nc.psum_tensor(f"ps_lnsq{c.ln_calls}", [1, 512], F32))
    ps_b1 = es2.enter_context(nc.psum_tensor(f"ps_lnb1{c.ln_calls}", [128, 512], F32))
    ps_b2 = es2.enter_context(nc.psum_tensor(f"ps_lnb2{c.ln_calls}", [128, 512], F32))
    rw1 = es2.enter_context(nc.sbuf_tensor(f"ln_rw1{c.ln_calls}", [1, 512], F32))
    rw2 = es2.enter_context(nc.sbuf_tensor(f"ln_rw2{c.ln_calls}", [1, 512], F32))
    rw3 = es2.enter_context(nc.sbuf_tensor(f"ln_rw3{c.ln_calls}", [1, 512], F32))
    rbA = es2.enter_context(nc.sbuf_tensor(f"ln_rbA{c.ln_calls}", [128, 512], F32))
    rbB = es2.enter_context(nc.sbuf_tensor(f"ln_rbB{c.ln_calls}", [128, 512], F32))
    lnb_w = lnb.shape[1] // KD
    prev_act_rows = None
    prev_pe_sq = None
    prev_dv_apply = None
    offs = []
    o = 0
    while o < ncols:
        if ncols - o < 512 and ncols >= 512:
            o = ncols - 512
        offs.append(o)
        o += 512
    for o in offs:
        wd = min(512, ncols - o)
        def xsl(buf, stride, off, width):
            return buf.ap().rearrange("p (t l) -> p t l", l=stride)[:, :, off:off+width]
        if src_dram is not None:
            def g1(g, o=o, wd=wd):
                for (sh, sn) in deps:
                    g.wait_ge(sh, sn)
                g.wait_ge(sm["ldm"].h, sm["ldm"].n)
                ii = g.dma_start(out=xsl(lnb, lnb_w, o, wd),
                                 in_=src_dram.rearrange("t p l -> p t l")[:, :, col0+o:col0+o+wd])
                sm["ldm"].inc(ii, 16)
            b.gpsimd(g1); ldm_cast = sm["ldm"].n
            def v1(v):
                v.wait_ge(sm["ldm"].h, ldm_cast)
                iv = v.engine_nop()
                sm["dv"].inc(iv)
            b.vector(v1); dv_cast = sm["dv"].n
        else:
            def v1(v, o=o, wd=wd):
                for (sh, sn) in deps:
                    v.wait_ge(sh, sn)
                iv = v.tensor_copy(out=xsl(lnb, lnb_w, o, wd), in_=xsl(xT, stride, col0 + o, wd))
                sm["dv"].inc(iv)
            b.vector(v1); dv_cast = sm["dv"].n
        def t1(t, o=o, wd=wd, war=prev_act_rows):
            t.wait_ge(sm["dv"].h, dv_cast)
            if war is not None:
                t.wait_ge(sm["act"].h, war)
            for k in range(KD):
                mm = t.matmul(ps_sum[0:1, 0:wd], sb["onescolb"][:],
                              lnb[:, k*lnb_w + o : k*lnb_w + o + wd],
                              start=(k == 0), stop=(k == KD - 1))
            sm["pe"].inc(mm)
        b.tensor(t1)
        def s1(s, o=o, wd=wd, war=prev_pe_sq):
            s.wait_ge(sm["dv"].h, dv_cast)
            if war is not None:
                s.wait_ge(sm["pe"].h, war)
            ia = s.activation(out=xsl(scr, 512, 0, wd), in_=xsl(lnb, lnb_w, o, wd), func=AF.Square)
            sm["act"].inc(ia)
        b.scalar(s1); act_sq = sm["act"].n
        def t2(t, o=o, wd=wd):
            t.wait_ge(sm["act"].h, act_sq)
            for k in range(KD):
                mm = t.matmul(ps_sq[0:1, 0:wd], sb["onescolb"][:],
                              scr[:, k*512 : k*512 + wd],
                              start=(k == 0), stop=(k == KD - 1))
            sm["pe"].inc(mm)
        b.tensor(t2); pe_sq = sm["pe"].n
        prev_pe_sq = pe_sq
        def s2(s, o=o, wd=wd, war=prev_dv_apply):
            s.wait_ge(sm["pe"].h, pe_sq)
            if war is not None:
                s.wait_ge(sm["dv"].h, war)
            s.mul(row_mu[0:1, col0+o-roff : col0+o-roff+wd], ps_sum[0:1, 0:wd], 1.0 / D)
            ia = s.mul(rw1[0:1, 0:wd], ps_sq[0:1, 0:wd], 1.0 / D)
            sm["act"].inc(ia)
        b.scalar(s2); act_rows = sm["act"].n
        prev_act_rows = act_rows
        def v2(v, o=o, wd=wd):
            v.wait_ge(sm["act"].h, act_rows)
            mu = row_mu[0:1, col0+o-roff : col0+o-roff+wd]
            iv = v.tensor_mul(rw2[0:1, 0:wd], mu, mu)
            sm["dv"].inc(iv); v.wait_ge(sm["dv"].h, sm["dv"].n)
            iv = v.tensor_sub(rw2[0:1, 0:wd], rw1[0:1, 0:wd], rw2[0:1, 0:wd])
            sm["dv"].inc(iv)
        b.vector(v2); dv_var = sm["dv"].n
        def s3(s, o=o, wd=wd):
            s.wait_ge(sm["dv"].h, dv_var)
            ia = s.activation(out=rw3[0:1, 0:wd], in_=rw2[0:1, 0:wd], func=AF.Sqrt, bias=sb["constv"][0:1, 0:1])
            sm["act"].inc(ia)
        b.scalar(s3); act_sd = sm["act"].n
        def v3(v, o=o, wd=wd):
            v.wait_ge(sm["act"].h, act_sd)
            iv = v.reciprocal(row_rstd[0:1, col0+o-roff : col0+o-roff+wd], rw3[0:1, 0:wd])
            sm["dv"].inc(iv); v.wait_ge(sm["dv"].h, sm["dv"].n)
            rstd = row_rstd[0:1, col0+o-roff : col0+o-roff+wd]
            mu = row_mu[0:1, col0+o-roff : col0+o-roff+wd]
            iv = v.tensor_mul(rw3[0:1, 0:wd], mu, rstd)
            sm["dv"].inc(iv); v.wait_ge(sm["dv"].h, sm["dv"].n)
            iv = v.tensor_scalar_mul(rw3[0:1, 0:wd], rw3[0:1, 0:wd], -1.0)
            sm["dv"].inc(iv)
        b.vector(v3); dv_rows = sm["dv"].n
        def t3(t, o=o, wd=wd, war=prev_dv_apply):
            t.wait_ge(sm["dv"].h, dv_rows)
            if war is not None:
                t.wait_ge(sm["dv"].h, war)
            t.matmul(ps_b1[:, 0:wd], sb["ones1f"][:], row_rstd[0:1, col0+o-roff : col0+o-roff+wd],
                     start=True, stop=True)
            mm = t.matmul(ps_b2[:, 0:wd], sb["ones1f"][:], rw3[0:1, 0:wd], start=True, stop=True)
            sm["pe"].inc(mm)
        b.tensor(t3); pe_bc = sm["pe"].n
        def v4(v, o=o, wd=wd):
            v.wait_ge(sm["pe"].h, pe_bc)
            v.tensor_copy(rbA[:, 0:wd], ps_b1[:, 0:wd])
            iv = v.tensor_copy(rbB[:, 0:wd], ps_b2[:, 0:wd])
            sm["dv"].inc(iv); v.wait_ge(sm["dv"].h, sm["dv"].n)
            iv = v.tensor_tensor(out=xsl(lnb, lnb_w, o, wd), in0=xsl(lnb, lnb_w, o, wd),
                                 in1=rbA[:, 0:wd].unsqueeze(1).broadcast_to([128, KD, wd]),
                                 op=OP.mult)
            sm["dv"].inc(iv); v.wait_ge(sm["dv"].h, sm["dv"].n)
            iv = v.tensor_tensor(out=xsl(lnb, lnb_w, o, wd), in0=xsl(lnb, lnb_w, o, wd),
                                 in1=rbB[:, 0:wd].unsqueeze(1).broadcast_to([128, KD, wd]),
                                 op=OP.add)
            sm["dv"].inc(iv)
        b.vector(v4)
        prev_dv_apply = sm["dv"].n
        o += 512


def emit_dump(c, pairs, final=True):
    nc, b, sm = c.nc, c.block, c.sm
    def g_body(g):
        w(g, sm["dv"]); w(g, sm["act"]); w(g, sm["pe"]); w(g, sm["pool"])
        for (dap, sbt, stride, col0, wd, nt) in pairs:
            src = sbt.ap().rearrange("p (t l) -> p t l", l=stride)[:, 0:nt, col0:col0+wd]
            ii = g.dma_start(out=dap.rearrange("t p l -> p t l"), in_=src)
            sm["sts"].inc(ii, 16)
        if final:
            g.wait_ge(sm["sts"].h, sm["sts"].n)
    b.gpsimd(g_body)


def emit_inproj(c, es2):
    pass


def _wslab_dma(c, g, slab, dram3d, wait_pe=None):
    sm = c.sm
    def body(gp):
        if wait_pe is not None:
            gp.wait_ge(sm["pe"].h, wait_pe)
        mcols = dram3d.shape[2]
        kk = dram3d.shape[1]
        gp.wait_ge(sm["ldw"].h, sm["ldw"].n)
        ii = gp.dma_start(out=slab[:, 0:kk*mcols].rearrange("p (k m) -> p k m", m=mcols),
                          in_=dram3d)
        sm["ldw"].inc(ii, 16)
    c.block.gpsimd(body)
    return sm["ldw"].n


def emit_zspill(c, es2):
    nc, b, sb, dr, sm = c.nc, c.block, c.sb, c.dr, c.sm
    lnb = c.lnb
    binzB = es2.enter_context(nc.sbuf_tensor("binzB", [128, D], F32))
    mupzB = es2.enter_context(nc.sbuf_tensor("mupzB", [128, D], F32))
    rowtmp = es2.enter_context(nc.sbuf_tensor("z_rowtmp", [1, 512], F32))
    ztmp = es2.enter_context(nc.sbuf_tensor("ztmp", [128, 512], F32))
    zst = [es2.enter_context(nc.sbuf_tensor(f"zst{i}", [128, 512], BF16)) for i in range(2)]
    wsl = c.wsl
    ps_bc = es2.enter_context(nc.psum_tensor("ps_zbc", [128, 512], F32))
    ps_z = [es2.enter_context(nc.psum_tensor(f"ps_z{i}", [128, 512], F32)) for i in range(2)]
    dv_b = None
    for j in range(4):
        for (dname, dstb) in (("bin_row", binzB), ("mup_row", mupzB)):
            def g_row(g, j=j, dname=dname, dv_b=dv_b):
                if dv_b is not None:
                    g.wait_ge(sm["dv"].h, dv_b)
                g.wait_ge(sm["ldm"].h, sm["ldm"].n)
                ii = g.dma_start(out=rowtmp[:], in_=dr[dname][0:1, j*512:(j+1)*512])
                sm["ldm"].inc(ii, 16)
            b.gpsimd(g_row); ldm_row = sm["ldm"].n
            def t_b(t, ldm_row=ldm_row, dv_b=dv_b):
                t.wait_ge(sm["ldm"].h, ldm_row)
                if dv_b is not None:
                    t.wait_ge(sm["dv"].h, dv_b)
                mm = t.matmul(ps_bc[:, 0:512], sb["ones1f"][:],
                              rowtmp[0:1, 0:512], start=True, stop=True)
                sm["pe"].inc(mm)
            b.tensor(t_b); pe_b = sm["pe"].n
            def v_b(v, j=j, dstb=dstb, pe_b=pe_b):
                v.wait_ge(sm["pe"].h, pe_b)
                iv = v.tensor_copy(dstb[:, j*512:(j+1)*512], ps_bc[:, 0:512])
                sm["dv"].inc(iv)
            b.vector(v_b); dv_b = sm["dv"].n
    wdram = dr["w_in"].rearrange("(k p) m -> p k m", p=128)
    ldw_t = {}
    pe_after = {}
    war = {}
    ldw_t[0] = _wslab_dma(c, 0, wsl[0], wdram[:, :, 0:512])
    for zc in range(4):
        if zc + 1 < 4:
            ldw_t[zc+1] = _wslab_dma(c, zc+1, wsl[(zc+1) % 2], wdram[:, :, (zc+1)*512:(zc+2)*512],
                                     wait_pe=pe_after.get(zc - 1))
        for tt in range(NCH):
            pz = ps_z[tt % 2]
            def t_z(t, zc=zc, tt=tt, pz=pz):
                t.wait_ge(sm["ldw"].h, ldw_t[zc])
                prev = war.get(("ps", tt % 2))
                if prev is not None:
                    t.wait_ge(sm["dv"].h, prev)
                for k in range(KD):
                    mm = t.matmul(pz[:], lnb[:, k*TT + 3 + 128*tt : k*TT + 3 + 128*tt + 128],
                                  wsl[zc % 2][:, k*512:(k+1)*512],
                                  start=(k == 0), stop=(k == KD - 1))
                sm["pe"].inc(mm)
            b.tensor(t_z); pe_z = sm["pe"].n
            pe_after[zc] = pe_z
            zb = zst[tt % 2]
            def v_z(v, zc=zc, pz=pz, zb=zb, pe_z=pe_z, tt=tt):
                v.wait_ge(sm["pe"].h, pe_z)
                prev = war.get(("zst", tt % 2))
                if prev is not None:
                    v.wait_ge(sm["spl"].h, prev)
                iv = v.tensor_tensor(ztmp[:], pz[:], binzB[:, zc*512:(zc+1)*512], op=OP.add)
                sm["dv"].inc(iv); v.wait_ge(sm["dv"].h, sm["dv"].n)
                iv = v.tensor_tensor(zb[:], ztmp[:], mupzB[:, zc*512:(zc+1)*512], op=OP.mult)
                sm["dv"].inc(iv)
            b.vector(v_z); dv_z = sm["dv"].n
            war[("ps", tt % 2)] = dv_z
            def g_z(g, zc=zc, tt=tt, zb=zb, dv_z=dv_z):
                g.wait_ge(sm["dv"].h, dv_z)
                g.wait_ge(sm["spl"].h, sm["spl"].n)
                ii = g.dma_start(out=c.z_dram[tt, :, zc*512:(zc+1)*512], in_=zb[:])
                sm["spl"].inc(ii, 16)
            b.gpsimd(g_z)
            war[("zst", tt % 2)] = sm["spl"].n


def emit_inproj_fmajor_half(c, h, xBCraw, es2, wsl, ldw_next):
    nc, b, sb, dr, sm = c.nc, c.block, c.sb, c.dr, c.sm
    lnb = c.lnb
    wdram = dr["w_in"].rearrange("(k p) m -> p k m", p=128)
    groups = [(2048 + 512*g, 512) for g in range(5)] + [(4608, 32)]
    ps = [es2.enter_context(nc.psum_tensor(f"ps_ip{h}{i}", [128, 512], F32)) for i in range(2)]
    ps_h = es2.enter_context(nc.psum_tensor(f"ps_iph{h}", [3, 512], F32)) if h == 0 else None
    ps_ht = es2.enter_context(nc.psum_tensor(f"ps_ipht{h}", [128, 4], F32)) if h == 0 else None
    halo_sb = es2.enter_context(nc.sbuf_tensor(f"halo_sb{h}", [3, 512], F32)) if h == 0 else None
    war = {}
    ldw_t = {}
    pe_after = {}
    col0 = 512 * h
    ldw_t[0] = _wslab_dma(c, 0, wsl[0], wdram[:, :, groups[0][0]:groups[0][0]+groups[0][1]],
                          wait_pe=ldw_next.get("pe_prev"))
    for gi, (m0, gw) in enumerate(groups):
        if gi + 1 < len(groups):
            m1, gw1 = groups[gi + 1]
            ldw_t[gi+1] = _wslab_dma(c, gi+1, wsl[(gi+1) % 2], wdram[:, :, m1:m1+gw1],
                                     wait_pe=pe_after.get(gi - 1))
        ntiles = ceil_div(gw, 128)
        slab = wsl[gi % 2]
        for mt in range(ntiles):
            tile = (m0 - 2048) // 128 + mt
            mw = min(128, gw - mt * 128)
            for (jo, jw) in ((0, 512), (3, 512)):
                p = ps[mt % 2]
                def t_m(t, gi=gi, mt=mt, jo=jo, jw=jw, p=p, slab=slab, mw=mw, gw=gw):
                    t.wait_ge(sm["ldw"].h, ldw_t[gi])
                    prev = war.get(("ps", mt % 2))
                    if prev is not None:
                        t.wait_ge(sm["dv"].h, prev)
                    for k in range(KD):
                        mm = t.matmul(p[0:mw, 0:jw],
                                      slab[:, k*gw + mt*128 : k*gw + mt*128 + mw],
                                      lnb[:, k*TT + col0 + jo : k*TT + col0 + jo + jw],
                                      start=(k == 0), stop=(k == KD - 1))
                    sm["pe"].inc(mm)
                b.tensor(t_m); pe_m = sm["pe"].n
                pe_after[gi] = pe_m
                def v_m(v, gi=gi, mt=mt, jo=jo, jw=jw, p=p, mw=mw, tile=tile, pe_m=pe_m):
                    v.wait_ge(sm["pe"].h, pe_m)
                    if tile < 20:
                        iv = v.tensor_scalar(out=xBCraw[0:mw, tile*515 + jo : tile*515 + jo + jw],
                                             in0=p[0:mw, 0:jw],
                                             scalar1=sb["mup_cols"][0:mw, 16+tile:17+tile],
                                             scalar2=sb["binmup"][0:mw, 16+tile:17+tile],
                                             op0=OP.mult, op1=OP.add)
                    else:
                        iv = v.tensor_scalar(out=sb["dtraw"][0:32, col0 + jo : col0 + jo + jw],
                                             in0=p[0:32, 0:jw],
                                             scalar1=sb["mup_cols"][0:32, 36:37],
                                             scalar2=sb["binmup"][0:32, 36:37],
                                             op0=OP.mult, op1=OP.add)
                    sm["dv"].inc(iv)
                b.vector(v_m)
                war[("ps", mt % 2)] = sm["dv"].n
        if h == 0 and m0 < 4608:
            def t_h(t, gi=gi, gw=gw, slab=slab):
                t.wait_ge(sm["ldw"].h, ldw_t[gi])
                prev = war.get("ps_h")
                if prev is not None:
                    t.wait_ge(sm["dv"].h, prev)
                for k in range(KD):
                    mm = t.matmul(ps_h[0:3, 0:gw], lnb[:, k*TT : k*TT + 3],
                                  slab[:, k*gw : k*gw + gw],
                                  start=(k == 0), stop=(k == KD - 1))
                sm["pe"].inc(mm)
            b.tensor(t_h); pe_h = sm["pe"].n
            pe_after[gi] = pe_h
            def v_h(v, pe_h=pe_h):
                v.wait_ge(sm["pe"].h, pe_h)
                iv = v.tensor_copy(halo_sb[:], ps_h[0:3, 0:512])
                sm["dv"].inc(iv)
            b.vector(v_h); dv_h = sm["dv"].n
            war["ps_h"] = dv_h
            for mt in range(4):
                tile = (m0 - 2048) // 128 + mt
                def t_ht(t, mt=mt, dv_h=dv_h):
                    t.wait_ge(sm["dv"].h, dv_h)
                    prev = war.get("ps_ht")
                    if prev is not None:
                        t.wait_ge(sm["dv"].h, prev)
                    mm = t.transpose(ps_ht[:, 0:3], halo_sb[0:3, mt*128:(mt+1)*128], sb["identf"][0:3, 0:3])
                    sm["pe"].inc(mm)
                b.tensor(t_ht); pe_ht = sm["pe"].n
                def v_ht(v, tile=tile, pe_ht=pe_ht):
                    v.wait_ge(sm["pe"].h, pe_ht)
                    iv = v.tensor_scalar(out=xBCraw[:, tile*515 : tile*515 + 3],
                                         in0=ps_ht[:, 0:3],
                                         scalar1=sb["mup_cols"][:, 16+tile:17+tile],
                                         scalar2=sb["binmup"][:, 16+tile:17+tile],
                                         op0=OP.mult, op1=OP.add)
                    sm["dv"].inc(iv); v.wait_ge(sm["dv"].h, sm["dv"].n)
                    iv = v.tensor_scalar(out=xBCraw[:, tile*515 : tile*515 + 3],
                                         in0=xBCraw[:, tile*515 : tile*515 + 3],
                                         scalar1=sb["halo_mask"][:, 0:1], scalar2=None,
                                         op0=OP.mult)
                    sm["dv"].inc(iv)
                b.vector(v_ht)
                war["ps_ht"] = sm["dv"].n
    ldw_next["pe_prev"] = max(pe_after.values())


def emit_conv_prep_light(c, h, xBCraw, esh):
    nc, b, sb, sm = c.nc, c.block, c.sb, c.sm
    xa_b = esh.enter_context(nc.sbuf_tensor(f"xa_b{h}", [128, KD * 512], BF16))
    bcst = esh.enter_context(nc.sbuf_tensor(f"bcst{h}", [128, 4 * 512], BF16))
    acc = [esh.enter_context(nc.sbuf_tensor(f"cacc{h}{i}", [128, 512], F32)) for i in range(2)]
    xtk = [esh.enter_context(nc.sbuf_tensor(f"xtk{h}{i}", [128, DSSM], BF16)) for i in range(2)]
    bgc = esh.enter_context(nc.sbuf_tensor(f"bgc{h}", [32, 1024], BF16))
    btk = esh.enter_context(nc.sbuf_tensor(f"btk{h}", [128, 256], BF16))
    bdc = esh.enter_context(nc.sbuf_tensor(f"bdc{h}", [128, 1024], BF16))
    xdt = esh.enter_context(nc.sbuf_tensor(f"xdtl{h}", [128, DSSM], BF16))
    srow = esh.enter_context(nc.sbuf_tensor(f"srow{h}", [1, 32], F32))
    ps_tr4 = esh.enter_context(nc.psum_tensor(f"ps_tr4{h}", [128, 512], BF16))
    ps_bt = esh.enter_context(nc.psum_tensor(f"ps_bt{h}", [128, 256], BF16))
    ps_smq = esh.enter_context(nc.psum_tensor(f"ps_smq{h}", [128, 512], F32))
    ps_smB = ps_smq[:, 64:96]
    emt = esh.enter_context(nc.sbuf_tensor(f"emt{h}", [128, 32], F32))
    ps_trf = ps_smq[:, 0:32]
    ps_r = ps_smq[0:1, 128:160]
    ps_eb = ps_smq[0:32, 256:288]
    ps_ct = esh.enter_context(nc.psum_tensor(f"ps_ct{h}", [32, DSSM], F32))
    war = {}
    for tile in range(20):
        a = acc[tile % 2]
        def v_cv(v, tile=tile, a=a):
            w(v, sm["dv"])
            if war.get(("acc", tile % 2)) is not None:
                v.wait_ge(sm["act"].h, war[("acc", tile % 2)])
            v.tensor_scalar(out=a[:], in0=xBCraw[:, tile*515 + 0 : tile*515 + 512],
                            scalar1=sb["convw"][:, tile*4 + 0 : tile*4 + 1], scalar2=None,
                            op0=OP.mult)
            for k in (1, 2, 3):
                iv = v.scalar_tensor_tensor(out=a[:], in0=xBCraw[:, tile*515 + k : tile*515 + k + 512],
                                            scalar=sb["convw"][:, tile*4 + k : tile*4 + k + 1],
                                            in1=a[:], op0=OP.mult, op1=OP.add)
                sm["dv"].inc(iv)
                v.wait_ge(sm["dv"].h, sm["dv"].n)
        b.vector(v_cv); dv_cv = sm["dv"].n
        def s_cv(s, tile=tile, a=a, dv_cv=dv_cv):
            s.wait_ge(sm["dv"].h, dv_cv)
            if tile < 16:
                dst = xa_b[:, tile*512:(tile+1)*512]
            else:
                dst = bcst[:, (tile-16)*512:(tile-15)*512]
            ia = s.activation(out=dst, in_=a[:], func=AF.Silu,
                              bias=sb["convb"][:, tile:tile+1])
            sm["act"].inc(ia)
        b.scalar(s_cv)
        war[("acc", tile % 2)] = sm["act"].n
    act_conv = sm["act"].n
    def g_bc(g, h=h):
        g.wait_ge(sm["act"].h, act_conv)
        for (src_t, dram) in ((0, c.bg_dram), (1, c.cg_dram)):
            for g8 in range(8):
                tile = src_t * 2 + g8 // 4
                base = 32 * (g8 % 4)
                g.wait_ge(sm["spl"].h, sm["spl"].n)
                ii = g.dma_start(out=dram[h, :, g8*512:(g8+1)*512],
                                 in_=bcst[base:base+32, tile*512:(tile+1)*512])
                sm["spl"].inc(ii, 16)
    b.gpsimd(g_bc); spl_bc = sm["spl"].n
    col0 = 512 * h + 3
    hcols = slice(512*h, 512*h + 512)
    def s_dt(s, h=h):
        w(s, sm["dv"])
        for ia in (
            s.activation(out=c.dt_r[0:32, 0:512], in_=sb["dtraw"][0:32, col0:col0+512],
                         func=AF.Relu, bias=sb["dtb"][0:32, 0:1]),):
            sm["act"].inc(ia); s.wait_ge(sm["act"].h, sm["act"].n)
        for fn, kw in ((AF.Abs, dict(bias=sb["dtb"][0:32, 0:1])),
                       (AF.Exp, dict(scale=-1.0)),
                       (AF.Ln, dict(bias=sb["constv"][0:32, 1:2]))):
            ia = s.activation(out=c.dt_a[0:32, 0:512],
                              in_=(sb["dtraw"][0:32, col0:col0+512] if fn == AF.Abs else c.dt_a[0:32, 0:512]),
                              func=fn, **kw)
            sm["act"].inc(ia); s.wait_ge(sm["act"].h, sm["act"].n)
    b.scalar(s_dt); act_dt = sm["act"].n
    def v_dt(v, h=h, act_dt=act_dt):
        v.wait_ge(sm["act"].h, act_dt)
        iv = v.tensor_add(c.dt2[0:32, hcols], c.dt_r[0:32, 0:512], c.dt_a[0:32, 0:512])
        sm["dv"].inc(iv); v.wait_ge(sm["dv"].h, sm["dv"].n)
        iv = v.tensor_scalar(out=c.dta[0:32, hcols], in0=c.dt2[0:32, hcols],
                             scalar1=sb["expA"][0:32, 0:1], scalar2=None, op0=OP.mult)
        sm["dv"].inc(iv); v.wait_ge(sm["dv"].h, sm["dv"].n)
        for cl in range(4):
            cc = slice(512*h + 128*cl, 512*h + 128*cl + 128)
            iv = v.tensor_tensor_scan(out=c.S_all[0:32, cc], data0=c.dta[0:32, cc],
                                      data1=sb["zeros32"][0:32, 0:128], initial=0.0,
                                      op0=OP.add, op1=OP.add)
            sm["dv"].inc(iv); v.wait_ge(sm["dv"].h, sm["dv"].n)
    b.vector(v_dt); dv_S = sm["dv"].n
    prev_srow_users = (0, 0)
    for cl in range(4):
        ch = 4 * h + cl
        cc0 = 512 * h + 128 * cl
        xb = xtk[cl % 2]
        def t_bar(t):
            w(t, sm["dv"])
        b.tensor(t_bar)
        for rnd in range(4):
            def t_xt(t, cl=cl, rnd=rnd):
                if rnd == 0:
                    t.wait_ge(sm["act"].h, act_conv)
                    if war.get(("xtk", cl % 2)) is not None:
                        t.wait_ge(sm["spl"].h, war[("xtk", cl % 2)])
                    if war.get("ps_ct") is not None:
                        t.wait_ge(sm["dv"].h, war["ps_ct"])
                prev = war.get("ps_tr4")
                if prev is not None:
                    t.wait_ge(sm["dv"].h, prev)
                for kk in range(4):
                    k = rnd * 4 + kk
                    mm = t.transpose(ps_tr4[:, kk*128:(kk+1)*128],
                                     xa_b[:, k*512 + 128*cl : k*512 + 128*cl + 128],
                                     sb["identb"][:])
                sm["pe"].inc(mm)
            b.tensor(t_xt); pe_xt = sm["pe"].n
            def v_xt(v, rnd=rnd, xb=xb, pe_xt=pe_xt):
                v.wait_ge(sm["pe"].h, pe_xt)
                iv = v.tensor_copy(xb[:, rnd*512:(rnd+1)*512], ps_tr4[:])
                sm["dv"].inc(iv)
            b.vector(v_xt)
            war["ps_tr4"] = sm["dv"].n
        dv_xt = sm["dv"].n
        def g_xt(g, ch=ch, xb=xb, dv_xt=dv_xt):
            g.wait_ge(sm["dv"].h, dv_xt)
            g.wait_ge(sm["spl"].h, sm["spl"].n)
            ii = g.dma_start(out=c.xtok_dram[ch], in_=xb[:])
            sm["spl"].inc(ii, 16)
        b.gpsimd(g_xt)
        war[("xtk", cl % 2)] = sm["spl"].n
        def t_dt(t, cc0=cc0, srow_u=prev_srow_users):
            t.wait_ge(sm["dv"].h, dv_S)
            prev = war.get("bkq")
            if prev is not None:
                t.wait_ge(sm["dv"].h, prev)
            mm = t.transpose(ps_trf, c.dt2[0:32, cc0:cc0+128], sb["identf"][0:32, 0:32])
            sm["pe"].inc(mm)
            prev = war.get("ps_r")
            if prev is not None:
                t.wait_ge(sm["dv"].h, prev)
            mm = t.transpose(ps_r, c.S_all[0:32, cc0+127:cc0+128],
                             sb["identf"][0:32, 0:32])
            sm["pe"].inc(mm)
        b.tensor(t_dt); pe_dt = sm["pe"].n
        def v_dt2(v, ch=ch, pe_dt=pe_dt, srow_u=prev_srow_users):
            v.wait_ge(sm["pe"].h, pe_dt)
            v.wait_ge(sm["pe"].h, srow_u[0])
            v.wait_ge(sm["act"].h, srow_u[1])
            v.tensor_copy(c.dt_tok[:, ch*32:(ch+1)*32], ps_trf)
            iv = v.tensor_copy(srow[0:1, 0:32], ps_r)
            sm["dv"].inc(iv)
        b.vector(v_dt2); dv_r = sm["dv"].n
        war["bkq"] = dv_r
        def s_ee(s, ch=ch, dv_r=dv_r):
            s.wait_ge(sm["dv"].h, dv_r)
            ia = s.activation(out=c.eend_row[0:1, ch*32:(ch+1)*32], in_=srow[0:1, 0:32],
                              func=AF.Exp, scale=-1.0)
            sm["act"].inc(ia)
        b.scalar(s_ee); act_ee = sm["act"].n
        def t_em(t, cc0=cc0, ch=ch, act_ee=act_ee):
            t.wait_ge(sm["dv"].h, war["bkq"])
            mm = t.transpose(ps_trf, c.S_all[0:32, cc0:cc0+128], sb["identf"][0:32, 0:32])
            sm["pe"].inc(mm)
            mm = t.matmul(ps_smB, sb["monesf"][:], srow[0:1, 0:32], start=True, stop=True)
            sm["pe"].inc(mm)
            t.wait_ge(sm["act"].h, act_ee)
            mm = t.matmul(ps_eb, sb["ones1f"][0:1, 0:32],
                          c.eend_row[0:1, ch*32:(ch+1)*32], start=True, stop=True)
            sm["pe"].inc(mm)
        b.tensor(t_em); pe_em = sm["pe"].n
        def v_em(v, ch=ch, pe_em=pe_em):
            v.wait_ge(sm["pe"].h, pe_em)
            iv = v.tensor_copy(emt[:], ps_smB)
            sm["dv"].inc(iv); v.wait_ge(sm["dv"].h, sm["dv"].n)
            iv = v.tensor_tensor(out=emt[:], in0=emt[:], in1=ps_trf, op=OP.add)
            sm["dv"].inc(iv)
        b.vector(v_em); dv_em = sm["dv"].n
        war["bkq"] = dv_em
        def s_em(s, ch=ch, dv_em=dv_em):
            s.wait_ge(sm["dv"].h, dv_em)
            ia = s.activation(out=c.expmat[:, ch*32:(ch+1)*32], in_=emt[:], func=AF.Exp)
            sm["act"].inc(ia)
        b.scalar(s_em); act_em = sm["act"].n
        prev_srow_users = (pe_em, act_em)
        def v_eb(v, ch=ch, pe_em=pe_em):
            v.wait_ge(sm["pe"].h, pe_em)
            iv = v.tensor_copy(c.eend_bc[0:32, ch*32:(ch+1)*32], ps_eb)
            sm["dv"].inc(iv)
        b.vector(v_eb)
        war["bkq"] = sm["dv"].n
        def v_et(v, ch=ch, act_ee=act_ee):
            v.wait_ge(sm["act"].h, act_ee)
            iv = v.tensor_mul(c.etot_row[0:1, 0:32], c.etot_row[0:1, 0:32],
                              c.eend_row[0:1, ch*32:(ch+1)*32])
            sm["dv"].inc(iv)
        b.vector(v_et)
        def g_bg(g, h=h, cl=cl, spl_bc=spl_bc):
            g.wait_ge(sm["spl"].h, spl_bc)
            if war.get("bgc") is not None:
                g.wait_ge(sm["pe"].h, war["bgc"])
            g.wait_ge(sm["ldm"].h, sm["ldm"].n)
            ii = g.dma_start(out=bgc[0:32, 0:1024],
                             in_=c.bg_dram[h].rearrange("p (g l) -> p g l", l=512)[:, :, 128*cl:128*cl+128])
            sm["ldm"].inc(ii, 16)
        b.gpsimd(g_bg); ldm_bg = sm["ldm"].n
        for rnd in range(2):
            def t_bt(t, rnd=rnd, ldm_bg=ldm_bg):
                t.wait_ge(sm["ldm"].h, ldm_bg)
                prev = war.get("ps_bt")
                if prev is not None:
                    t.wait_ge(sm["dv"].h, prev)
                for gg in range(4):
                    g8 = rnd * 4 + gg
                    mm = t.transpose(ps_bt[:, gg*32:(gg+1)*32], bgc[0:32, g8*128:(g8+1)*128],
                                     sb["identb"][0:32, 0:32])
                sm["pe"].inc(mm)
            b.tensor(t_bt); pe_bt = sm["pe"].n
            def v_bt(v, rnd=rnd, pe_bt=pe_bt):
                v.wait_ge(sm["pe"].h, pe_bt)
                iv = v.tensor_copy(btk[:, rnd*128:(rnd+1)*128], ps_bt[:, 0:128])
                sm["dv"].inc(iv)
            b.vector(v_bt)
            war["ps_bt"] = sm["dv"].n
        dv_bt = sm["dv"].n
        war["bgc"] = sm["pe"].n
        def v_bd(v, ch=ch, cl=cl, xb=xb, dv_bt=dv_bt, act_em=act_em):
            v.wait_ge(sm["dv"].h, dv_bt)
            v.wait_ge(sm["act"].h, act_em)
            if war.get("bdc") is not None:
                v.wait_ge(sm["spl"].h, war["bdc"][0])
                v.wait_ge(sm["pe"].h, war["bdc"][1])
            for hh in range(NH):
                iv = v.tensor_scalar(out=bdc[:, hh*32:(hh+1)*32], in0=btk[:, (hh//4)*32:(hh//4+1)*32],
                                     scalar1=c.expmat[:, ch*32+hh:ch*32+hh+1], scalar2=None,
                                     op0=OP.mult)
                sm["dv"].inc(iv)
                v.wait_ge(sm["dv"].h, sm["dv"].n)
            dtb = c.dt_tok[:, ch*32:(ch+1)*32]
            iv = v.tensor_tensor(out=xdt.ap().rearrange("p (h q) -> p h q", q=PH),
                                 in0=xb.ap().rearrange("p (h q) -> p h q", q=PH),
                                 in1=dtb.unsqueeze(2).broadcast_to([128, 32, PH]), op=OP.mult)
            sm["dv"].inc(iv)
        b.vector(v_bd); dv_bd = sm["dv"].n
        def g_bd(g, ch=ch, dv_bd=dv_bd):
            g.wait_ge(sm["dv"].h, dv_bd)
            g.wait_ge(sm["spl"].h, sm["spl"].n)
            ii = g.dma_start(out=c.bd_dram[ch], in_=bdc[:])
            sm["spl"].inc(ii, 16)
        b.gpsimd(g_bd)
        def t_ct(t, ch=ch, dv_bd=dv_bd):
            t.wait_ge(sm["dv"].h, dv_bd)
            prev = war.get("ps_ct")
            if prev is not None:
                t.wait_ge(sm["dv"].h, prev)
            for hh in range(NH):
                mm = t.matmul(ps_ct[0:32, hh*64:(hh+1)*64], bdc[:, hh*32:(hh+1)*32],
                              xdt[:, hh*64:(hh+1)*64], start=True, stop=True)
            sm["pe"].inc(mm)
        b.tensor(t_ct); pe_ct = sm["pe"].n
        war["bdc"] = (sm["spl"].n, pe_ct)
        def v_hl(v, ch=ch, pe_ct=pe_ct):
            v.wait_ge(sm["pe"].h, pe_ct)
            for hh in range(NH):
                iv = v.scalar_tensor_tensor(out=c.Hloc[0:32, hh*64:(hh+1)*64],
                                            in0=c.Hloc[0:32, hh*64:(hh+1)*64],
                                            scalar=c.eend_bc[0:32, ch*32+hh:ch*32+hh+1],
                                            in1=ps_ct[0:32, hh*64:(hh+1)*64],
                                            op0=OP.mult, op1=OP.add)
                sm["dv"].inc(iv)
                v.wait_ge(sm["dv"].h, sm["dv"].n)
        b.vector(v_hl)
        war["ps_ct"] = sm["dv"].n
        if "tapchunks" in getattr(c, "dbg_skip", set()) and "tapH" in c.dbg:
            def g_tc(g, ch=ch, cl=cl, h=h):
                for s2 in ("pe", "act", "dv", "spl", "ldm", "ldw"):
                    w(g, sm[s2])
                g.wait_ge(sm["sts"].h, sm["sts"].n)
                ii = g.dma_start(out=c.dbg["tapH"][ch % 4], in_=c.Hloc[0:32, :])
                sm["sts"].inc(ii, 16)
                g.wait_ge(sm["sts"].h, sm["sts"].n)
                ii = g.dma_start(out=c.dbg["y1"][8 + (ch % 4)][0:128, 0:1024], in_=xdt[:, 0:2048].bitcast(F32))
                sm["sts"].inc(ii, 16)
                g.wait_ge(sm["sts"].h, sm["sts"].n)
                ii = g.dma_start(out=c.dbg["y1"][12][0:128, 0:512], in_=bdc[:, 0:1024].bitcast(F32))
                sm["sts"].inc(ii, 16)
                g.wait_ge(sm["sts"].h, sm["sts"].n)
                ii = g.dma_start(out=c.dbg["y1"][13][0:128, 0:128], in_=btk[:, 0:256].bitcast(F32))
                sm["sts"].inc(ii, 16)
                g.wait_ge(sm["sts"].h, sm["sts"].n)
                ii = g.dma_start(out=c.dbg["y1"][14][0:32, 0:512], in_=bgc[0:32, 0:1024].bitcast(F32))
                sm["sts"].inc(ii, 16)
            b.gpsimd(g_tc)


def emit_state_exchange(c, es2):
    nc, b, sb, sm = c.nc, c.block, c.sb, c.sm
    htmp = es2.enter_context(nc.sbuf_tensor("htmp", [32, DSSM], F32))
    erow_all = es2.enter_context(nc.sbuf_tensor("erow_all", [1, 256], F32))
    etp = es2.enter_context(nc.sbuf_tensor("etp", [1, 32], F32))
    omw = es2.enter_context(nc.sbuf_tensor("omw", [1, 8], F32))
    rankw_bc = es2.enter_context(nc.sbuf_tensor("rankw_bc", [32, 8], F32))
    etot_bc = es2.enter_context(nc.sbuf_tensor("etot_bc", [32, 32], F32))
    ps_s = es2.enter_context(nc.psum_tensor("ps_se", [32, 32], F32))
    dv0 = sm["dv"].n
    def g_out(g):
        w(g, sm["dv"])
        g.wait_ge(sm["spl"].h, sm["spl"].n)
        ii = g.dma_start(out=c.stag_in.ap()[0:32, :], in_=c.Hloc[0:32, :])
        sm["spl"].inc(ii, 16)
        ii = g.dma_start(out=c.stag_in.ap()[32:33, 0:32], in_=c.etot_row[0:1, 0:32])
        sm["spl"].inc(ii, 16)
        g.wait_ge(sm["spl"].h, sm["spl"].n)
        c.spl_in_mark = sm["spl"].n
        ii = g.collective_compute("AllGather", OP.bypass, replica_groups=[list(range(NC))],
                                  ins=[c.stag_in.ap().opt()], outs=[c.stag_out.ap().opt()])
        sm["cc"].inc(ii, 1)
        g.wait_ge(sm["cc"].h, sm["cc"].n)
        src = c.stag_out.ap().rearrange("(r q) m -> r q m", q=33)[:, 32, 0:32]
        g.wait_ge(sm["ldm"].h, sm["ldm"].n)
        ii = g.dma_start(out=erow_all[0:1, :].rearrange("p (r m) -> p r m", m=32),
                         in_=src.unsqueeze(0))
        sm["ldm"].inc(ii, 16)
    b.gpsimd(g_out)
    spl_in = c.spl_in_mark
    ldm_e = sm["ldm"].n
    def v_pr(v):
        v.wait_ge(sm["ldm"].h, ldm_e)
        iv = v.tensor_scalar(out=omw[0:1, 0:8], in0=sb["rank_w"][0:1, 0:8],
                             scalar1=-1.0, scalar2=1.0, op0=OP.mult, op1=OP.add)
        sm["dv"].inc(iv)
    b.vector(v_pr); dv_pr = sm["dv"].n
    def t_rw(t):
        t.wait_ge(sm["dv"].h, dv_pr)
        mm = t.matmul(ps_s[0:32, 0:8], sb["ones1f"][0:1, 0:32], sb["rank_w"][0:1, 0:8],
                      start=True, stop=True)
        sm["pe"].inc(mm)
    b.tensor(t_rw); pe_rw = sm["pe"].n
    def v_rw(v):
        v.wait_ge(sm["pe"].h, pe_rw)
        v.wait_ge(sm["spl"].h, spl_in)
        iv = v.tensor_copy(rankw_bc[0:32, 0:8], ps_s[0:32, 0:8])
        sm["dv"].inc(iv); v.wait_ge(sm["dv"].h, sm["dv"].n)
        iv = v.memset(c.Hcur[0:32, :], 0.0)
        sm["dv"].inc(iv)
    b.vector(v_rw)
    if "tapmemset" in getattr(c, "dbg_skip", set()):
        dv_ms = sm["dv"].n
        def g_ms(g):
            g.wait_ge(sm["dv"].h, dv_ms)
            g.wait_ge(sm["sts"].h, sm["sts"].n)
            ii = g.dma_start(out=c.dbg["tapH"][3], in_=c.Hcur[0:32, :])
            sm["sts"].inc(ii, 16)
        b.gpsimd(g_ms)
    war_ps = sm["dv"].n
    for r in range(NC):
        def g_h(g, r=r):
            if r >= 1:
                w(g, sm["dv"])
            g.wait_ge(sm["ldm"].h, sm["ldm"].n)
            ii = g.dma_start(out=htmp[0:32, :],
                             in_=c.stag_out.ap().rearrange("(r q) m -> r q m", q=33)[r, 0:32, :])
            sm["ldm"].inc(ii, 16)
        b.gpsimd(g_h); ldm_h = sm["ldm"].n
        def v_e(v, r=r, ldm_h=ldm_h):
            v.wait_ge(sm["ldm"].h, ldm_h)
            iv = v.tensor_scalar(out=etp[0:1, 0:32], in0=erow_all[0:1, r*32:(r+1)*32],
                                 scalar1=sb["rank_w"][0:1, r:r+1], scalar2=omw[0:1, r:r+1],
                                 op0=OP.mult, op1=OP.add)
            sm["dv"].inc(iv); v.wait_ge(sm["dv"].h, sm["dv"].n)
            iv = v.tensor_scalar(out=htmp[0:32, :], in0=htmp[0:32, :],
                                 scalar1=rankw_bc[0:32, r:r+1], scalar2=None, op0=OP.mult)
            sm["dv"].inc(iv); v.wait_ge(sm["dv"].h, sm["dv"].n)
        b.vector(v_e); dv_e = sm["dv"].n
        def t_e(t, r=r, dv_e=dv_e, war_ps=war_ps):
            t.wait_ge(sm["dv"].h, dv_e)
            mm = t.matmul(ps_s[0:32, 0:32], sb["ones1f"][0:1, 0:32], etp[0:1, 0:32],
                          start=True, stop=True)
            sm["pe"].inc(mm)
        b.tensor(t_e); pe_e = sm["pe"].n
        def v_c(v, r=r, pe_e=pe_e):
            v.wait_ge(sm["pe"].h, pe_e)
            v.tensor_copy(etot_bc[0:32, 0:32], ps_s[0:32, 0:32])
            v.engine_nop()
            for hh in range(NH):
                iv = v.scalar_tensor_tensor(out=c.Hcur[0:32, hh*64:(hh+1)*64],
                                            in0=c.Hcur[0:32, hh*64:(hh+1)*64],
                                            scalar=etot_bc[0:32, hh:hh+1],
                                            in1=htmp[0:32, hh*64:(hh+1)*64],
                                            op0=OP.mult, op1=OP.add)
                sm["dv"].inc(iv)
                v.wait_ge(sm["dv"].h, sm["dv"].n)
        b.vector(v_c)
        war_ps = sm["dv"].n
    c.xtap = dict(erow=erow_all, etp=etp, omw=omw, rbc=rankw_bc, ebc=etot_bc, htmp=htmp)


def emit_mamba_main_half(c, h, esh):
    nc, b, sb, sm = c.nc, c.block, c.sb, c.sm
    zs2 = [esh.enter_context(nc.sbuf_tensor(f"zs{h}{i}", [128, D], BF16)) for i in range(2)]
    xtk = [esh.enter_context(nc.sbuf_tensor(f"mxtk{h}{i}", [128, DSSM], BF16)) for i in range(2)]
    bdc = [esh.enter_context(nc.sbuf_tensor(f"mbdc{h}{i}", [128, 1024], BF16)) for i in range(2)]
    bgc = [esh.enter_context(nc.sbuf_tensor(f"mbgc{h}{i}", [32, 1024], BF16)) for i in range(2)]
    cgc = [esh.enter_context(nc.sbuf_tensor(f"mcgc{h}{i}", [32, 1024], BF16)) for i in range(2)]
    xdt = esh.enter_context(nc.sbuf_tensor(f"mxdt{h}", [128, DSSM], BF16))
    xDb = esh.enter_context(nc.sbuf_tensor(f"mxD{h}", [128, DSSM], BF16))
    zsb = esh.enter_context(nc.sbuf_tensor(f"mzs{h}", [128, D], BF16))
    hbb = esh.enter_context(nc.sbuf_tensor(f"mhb{h}", [32, DSSM], BF16))
    gm = [esh.enter_context(nc.sbuf_tensor(f"mgm{h}{i}", [128, 128], BF16)) for i in range(2)]
    esb = [esh.enter_context(nc.sbuf_tensor(f"mesb{h}{i}", [128, 128], F32)) for i in range(2)]
    msb = [esh.enter_context(nc.sbuf_tensor(f"mmsb{h}{i}", [128, 128], BF16)) for i in range(2)]
    cxp = [esh.enter_context(nc.sbuf_tensor(f"mcxp{h}{i}", [32, 128], BF16)) for i in range(2)]
    shr = [esh.enter_context(nc.sbuf_tensor(f"mshr{h}{i}", [1, 128], F32)) for i in range(2)]
    nsr = [esh.enter_context(nc.sbuf_tensor(f"mnsr{h}{i}", [1, 128], F32)) for i in range(2)]
    ysc = esh.enter_context(nc.sbuf_tensor(f"mysc{h}", [128, DSSM], F32))
    ssq = esh.enter_context(nc.sbuf_tensor(f"mssq{h}", [128, 8], F32))
    rgn = esh.enter_context(nc.sbuf_tensor(f"mrgn{h}", [128, 8], F32))
    y4T = esh.enter_context(nc.sbuf_tensor(f"my4T{h}", [128, KD * 512], BF16))
    c.wsl2 = [esh.enter_context(nc.sbuf_tensor(f"wsl2_{h}{i}", [128, KD * 256], BF16)) for i in range(2)]
    ps_y = esh.enter_context(nc.psum_tensor(f"ps_y{h}", [128, DSSM], F32))
    ps_sm = [esh.enter_context(nc.psum_tensor(f"ps_sm{h}{i}", [128, 512], F32)) for i in range(2)]
    ps_ct = esh.enter_context(nc.psum_tensor(f"ps_ctm{h}", [32, 1024], F32))
    GSL, DSL, CSL, SSL = slice(0, 128), slice(128, 256), slice(256, 384), slice(384, 512)
    war = {}
    ldm_c = {}

    def prefetch(ch, buf):
        def g_pf(g, ch=ch, buf=buf):
            prev = war.get(("bufs", buf))
            if prev is not None:
                g.wait_ge(sm["pe"].h, prev[0])
                g.wait_ge(sm["dv"].h, prev[1])
                g.wait_ge(sm["act"].h, prev[2])
            g.wait_ge(sm["ldm"].h, sm["ldm"].n)
            for (dst, src) in ((zs2[buf].ap(), c.z_dram[ch]), (xtk[buf].ap(), c.xtok_dram[ch]),
                               (bdc[buf].ap(), c.bd_dram[ch])):
                ii = g.dma_start(out=dst, in_=src)
                sm["ldm"].inc(ii, 16)
            for (dst, dram) in ((bgc[buf], c.bg_dram), (cgc[buf], c.cg_dram)):
                ii = g.dma_start(out=dst[0:32, :],
                                 in_=dram[ch // 4].rearrange("p (g l) -> p g l", l=512)[:, :, 128*(ch % 4):128*(ch % 4)+128])
                sm["ldm"].inc(ii, 16)
        b.gpsimd(g_pf)
        ldm_c[ch] = sm["ldm"].n

    nheads = getattr(c, "dbg_heads", NH)
    skip = getattr(c, "dbg_skip", set())
    nchunks = getattr(c, "dbg_chunks", 4)
    prefetch(4 * h, 0)
    for cl in range(nchunks):
        ch = 4 * h + cl
        buf = cl % 2
        if cl + 1 < 4:
            prefetch(ch + 1, 1 - buf)
        cc0 = 512 * h + 128 * cl
        def v_pp(v, ch=ch, buf=buf):
            v.wait_ge(sm["ldm"].h, ldm_c[ch])
            prev = war.get("xdt_cons")
            if prev is not None:
                v.wait_ge(sm["pe"].h, prev)
            dtb = c.dt_tok[:, ch*32:(ch+1)*32]
            v.tensor_tensor(out=xdt.ap().rearrange("p (q u) -> p q u", u=PH),
                            in0=xtk[buf].ap().rearrange("p (q u) -> p q u", u=PH),
                            in1=dtb.unsqueeze(2).broadcast_to([128, 32, PH]), op=OP.mult)
            v.tensor_tensor(out=xDb[:], in0=xtk[buf][:], in1=c.DbcTok[:], op=OP.mult)
            iv = v.tensor_copy(hbb[0:32, :], c.Hcur[0:32, :])
            sm["dv"].inc(iv)
        b.vector(v_pp); dv_pp = sm["dv"].n
        if skip and ("sh" in skip or "cexp" in skip):
            def v_ms(v):
                iv = v.memset(msb[0][:], 0.001)
                iv = v.memset(msb[1][:], 0.001)
                iv = v.memset(cxp[0][0:32, :], 0.001)
                iv = v.memset(cxp[1][0:32, :], 0.001)
                sm["dv"].inc(iv)
            b.vector(v_ms)
        def s_z(s, ch=ch, buf=buf):
            s.wait_ge(sm["ldm"].h, ldm_c[ch])
            prev = war.get("zsb_cons")
            if prev is not None:
                s.wait_ge(sm["dv"].h, prev)
            ia = s.activation(out=zsb[:], in_=zs2[buf][:], func=AF.Silu)
            sm["act"].inc(ia)
        b.scalar(s_z); act_z = sm["act"].n
        pe_y_mms = None
        for hh in range(nheads):
            g8 = hh // 4
            pp = hh % 2
            gp = g8 % 2
            if hh % 4 == 0:
                def t_g(t, g8=g8, gp=gp, ch=ch, buf=buf):
                    t.wait_ge(sm["ldm"].h, ldm_c[ch])
                    prev = war.get(("bk", gp))
                    if prev is not None:
                        t.wait_ge(sm["dv"].h, prev)
                    mm = t.matmul(ps_sm[gp][:, GSL], bgc[buf][0:32, g8*128:(g8+1)*128],
                                  cgc[buf][0:32, g8*128:(g8+1)*128], start=True, stop=True)
                    sm["pe"].inc(mm)
                b.tensor(t_g); pe_g = sm["pe"].n
                def v_g(v, gp=gp, pe_g=pe_g):
                    v.wait_ge(sm["pe"].h, pe_g)
                    prev = war.get(("gm", gp))
                    if prev is not None:
                        v.wait_ge(sm["dv"].h, prev)
                    iv = v.tensor_tensor(out=gm[gp][:], in0=ps_sm[gp][:, GSL],
                                         in1=sb["caus01b"][:], op=OP.mult)
                    sm["dv"].inc(iv)
                b.vector(v_g)
                war[("bk", gp)] = sm["dv"].n
                war[("gmw", gp)] = sm["dv"].n
            if "sh" in skip:
                dv_sh = sm["dv"].n; act_ns = sm["act"].n; dv_m = sm["dv"].n
                dv_clp = sm["dv"].n; act_e = sm["act"].n
            def t_sh(t, hh=hh, pp=pp, cc0=cc0):
                prev = war.get(("bk", pp))
                if prev is not None:
                    t.wait_ge(sm["dv"].h, prev)
                mm = t.matmul(ps_sm[pp][0:1, SSL], sb["identf"][0:32, hh:hh+1],
                              c.S_all[0:32, cc0:cc0+128], start=True, stop=True)
                sm["pe"].inc(mm)
            if "sh" not in skip:
                b.tensor(t_sh)
            pe_sh = sm["pe"].n
            def v_sh(v, pp=pp, pe_sh=pe_sh):
                v.wait_ge(sm["pe"].h, pe_sh)
                iv = v.tensor_copy(shr[pp][0:1, :], ps_sm[pp][0:1, SSL])
                sm["dv"].inc(iv)
            if "sh" not in skip:
                b.vector(v_sh); dv_sh = sm["dv"].n
                war[("bk", pp)] = dv_sh
            def s_ns(s, pp=pp, dv_sh=dv_sh):
                s.wait_ge(sm["dv"].h, dv_sh)
                ia = s.activation(out=nsr[pp][0:1, :], in_=shr[pp][0:1, :], func=AF.Exp, scale=-1.0)
                sm["act"].inc(ia)
            if "sh" not in skip:
                b.scalar(s_ns); act_ns = sm["act"].n
            def t_d(t, pp=pp, dv_sh=dv_sh):
                t.wait_ge(sm["dv"].h, dv_sh)
                prev = war.get(("bk", pp))
                if prev is not None:
                    t.wait_ge(sm["dv"].h, prev)
                t.matmul(ps_sm[pp][:, DSL], shr[pp][0:1, :], sb["ones1f"][:], start=True, stop=False)
                mm = t.matmul(ps_sm[pp][:, DSL], sb["monesf"][:], shr[pp][0:1, :], start=False, stop=True)
                sm["pe"].inc(mm)
            if "sh" not in skip and "delta" not in skip:
                b.tensor(t_d)
            pe_d = sm["pe"].n
            def v_clp(v, pp=pp, pe_d=pe_d):
                v.wait_ge(sm["pe"].h, pe_d)
                iv = v.tensor_scalar(out=esb[pp][:], in0=ps_sm[pp][:, DSL], scalar1=0.0,
                                     scalar2=None, op0=OP.min)
                sm["dv"].inc(iv)
            if "sh" not in skip and "delta" not in skip:
                b.vector(v_clp); dv_clp = sm["dv"].n
                war[("bk", pp)] = dv_clp
            def s_e(s, pp=pp, dv_clp=dv_clp):
                s.wait_ge(sm["dv"].h, dv_clp)
                ia = s.activation(out=esb[pp][:], in_=esb[pp][:], func=AF.Exp)
                sm["act"].inc(ia)
            if "sh" not in skip and "delta" not in skip:
                b.scalar(s_e); act_e = sm["act"].n
            def v_m(v, pp=pp, gp=gp, act_e=act_e):
                v.wait_ge(sm["act"].h, act_e)
                v.wait_ge(sm["dv"].h, war[("gmw", gp)])
                prev = war.get(("msb", pp))
                if prev is not None:
                    v.wait_ge(sm["pe"].h, prev)
                iv = v.tensor_tensor(out=msb[pp][:], in0=esb[pp][:], in1=gm[gp][:], op=OP.mult)
                sm["dv"].inc(iv)
            if "sh" not in skip and "delta" not in skip:
                b.vector(v_m); dv_m = sm["dv"].n
            elif "delta" in skip and "sh" not in skip:
                dv_m = sm["dv"].n
            def t_c(t, pp=pp, act_ns=act_ns):
                t.wait_ge(sm["act"].h, act_ns)
                prev = war.get(("bk", pp))
                if prev is not None:
                    t.wait_ge(sm["dv"].h, prev)
                mm = t.matmul(ps_sm[pp][0:32, CSL], sb["ones1f"][0:1, 0:32], nsr[pp][0:1, :],
                              start=True, stop=True)
                sm["pe"].inc(mm)
            if "cexp" not in skip and "sh" not in skip:
                b.tensor(t_c)
            pe_c = sm["pe"].n
            def v_cx(v, pp=pp, g8=g8, pe_c=pe_c, buf=buf):
                v.wait_ge(sm["pe"].h, pe_c)
                prev = war.get(("cxp", pp))
                if prev is not None:
                    v.wait_ge(sm["pe"].h, prev)
                iv = v.tensor_tensor(out=cxp[pp][0:32, :], in0=cgc[buf][0:32, g8*128:(g8+1)*128],
                                     in1=ps_sm[pp][0:32, CSL], op=OP.mult)
                sm["dv"].inc(iv)
            if "cexp" not in skip and "sh" not in skip:
                b.vector(v_cx); dv_cx = sm["dv"].n
                war[("bk", pp)] = dv_cx
            else:
                dv_cx = sm["dv"].n
            def t_y(t, hh=hh, pp=pp, dv_m=dv_m, dv_cx=dv_cx, dv_pp=dv_pp):
                t.wait_ge(sm["dv"].h, dv_pp)
                t.wait_ge(sm["dv"].h, dv_m)
                t.wait_ge(sm["dv"].h, dv_cx)
                if hh == 0:
                    prev = war.get("ps_y")
                    if prev is not None:
                        t.wait_ge(sm["dv"].h, prev)
                mm = t.matmul(ps_y[:, hh*64:(hh+1)*64], sb["identb"][:], xDb[:, hh*64:(hh+1)*64],
                              start=True, stop=("y1mm" in skip))
                if "y1mm" not in skip:
                    if "nointra" not in skip:
                        mm = t.matmul(ps_y[:, hh*64:(hh+1)*64], msb[pp][:], xdt[:, hh*64:(hh+1)*64],
                                      start=False, stop=("nointer" in skip))
                    if "nointer" not in skip:
                        mm = t.matmul(ps_y[:, hh*64:(hh+1)*64], cxp[pp][0:32, :], hbb[0:32, hh*64:(hh+1)*64],
                                      start=False, stop=True)
                sm["pe"].inc(mm)
            if "ymm" not in skip:
                b.tensor(t_y)
            war[("msb", pp)] = sm["pe"].n
            war[("cxp", pp)] = sm["pe"].n
            pe_y_mms = sm["pe"].n
            def t_ctm(t, hh=hh, dv_pp=dv_pp, ch=ch, buf=buf):
                t.wait_ge(sm["dv"].h, dv_pp)
                t.wait_ge(sm["ldm"].h, ldm_c[ch])
                if hh % 16 == 0:
                    prev = war.get("ps_ctm")
                    if prev is not None:
                        t.wait_ge(sm["dv"].h, prev)
                mm = t.matmul(ps_ct[0:32, (hh % 16)*64:(hh % 16 + 1)*64],
                              bdc[buf][:, hh*32:(hh+1)*32], xdt[:, hh*64:(hh+1)*64],
                              start=True, stop=True)
                sm["pe"].inc(mm)
            if "ctm" not in skip:
                b.tensor(t_ctm)
            pe_ctm = sm["pe"].n
            if hh % 16 == 15 and "hu" not in skip:
                def v_hu(v, hh=hh, ch=ch, pe_ctm=pe_ctm):
                    v.wait_ge(sm["pe"].h, pe_ctm)
                    for q in range(16):
                        hq = (hh // 16) * 16 + q
                        iv = v.scalar_tensor_tensor(out=c.Hnxt[0:32, hq*64:(hq+1)*64],
                                                    in0=c.Hcur[0:32, hq*64:(hq+1)*64],
                                                    scalar=c.eend_bc[0:32, ch*32+hq:ch*32+hq+1],
                                                    in1=ps_ct[0:32, q*64:(q+1)*64],
                                                    op0=OP.mult, op1=OP.add)
                        sm["dv"].inc(iv)
                        v.wait_ge(sm["dv"].h, sm["dv"].n)
                b.vector(v_hu)
                war["ps_ctm"] = sm["dv"].n
        if "yevac" in skip:
            def v_skip(v, pe_y=pe_y_mms):
                v.wait_ge(sm["pe"].h, pe_y)
                iv = v.memset(ysc[:], 0.0)
                sm["dv"].inc(iv)
            b.vector(v_skip)
            war["ps_y"] = sm["dv"].n
            war["zsb_cons"] = sm["dv"].n
            war["xdt_cons"] = pe_y_mms
            war[("bufs", buf)] = (pe_y_mms, sm["dv"].n, sm["act"].n)
            c.Hcur, c.Hnxt = c.Hnxt, c.Hcur
            continue
        def v_y2(v, pe_y=pe_y_mms, act_z=act_z):
            v.wait_ge(sm["pe"].h, pe_y)
            v.wait_ge(sm["act"].h, act_z)
            if "rawy" in skip:
                iv = v.tensor_copy(ysc[:], ps_y[:])
            else:
                iv = v.tensor_tensor(out=ysc[:], in0=ps_y[:], in1=zsb[:], op=OP.mult)
            sm["dv"].inc(iv)
        b.vector(v_y2); dv_y2 = sm["dv"].n
        war["zsb_cons"] = dv_y2
        war["xdt_cons"] = pe_y_mms
        def s_sq(s, dv_y2=dv_y2):
            s.wait_ge(sm["dv"].h, dv_y2)
            for g8 in range(8):
                ia = s.activation(out=zsb[:, 0:256], in_=ysc[:, g8*256:(g8+1)*256], func=AF.Square,
                                  accum_out=ssq[:, g8:g8+1])
                sm["act"].inc(ia)
                s.wait_ge(sm["act"].h, sm["act"].n)
            ia = s.activation(out=rgn[:], in_=ssq[:], func=AF.Sqrt, scale=1.0/256.0,
                              bias=sb["constv"][:, 0:1])
            sm["act"].inc(ia)
        b.scalar(s_sq); act_sq = sm["act"].n
        def v_gn(v, act_sq=act_sq):
            v.wait_ge(sm["act"].h, act_sq)
            if "rawy" in skip:
                iv = v.engine_nop()
                sm["dv"].inc(iv)
                return
            iv = v.reciprocal(rgn[:], rgn[:])
            sm["dv"].inc(iv)
            v.wait_ge(sm["dv"].h, sm["dv"].n)
            for g8 in range(8):
                iv = v.tensor_scalar(out=ysc[:, g8*256:(g8+1)*256], in0=ysc[:, g8*256:(g8+1)*256],
                                     scalar1=rgn[:, g8:g8+1], scalar2=None, op0=OP.mult)
                sm["dv"].inc(iv)
                v.wait_ge(sm["dv"].h, sm["dv"].n)
            iv = v.tensor_tensor(out=ysc[:], in0=ysc[:], in1=c.normwB[:], op=OP.mult)
            sm["dv"].inc(iv)
        b.vector(v_gn); dv_gn = sm["dv"].n
        pe_prev_t4 = None
        for rnd in range(4):
            def t_t4(t, rnd=rnd, dv_gn=dv_gn):
                t.wait_ge(sm["dv"].h, dv_gn)
                for key in ("ps_t4", ("bk", 0)):
                    prev = war.get(key)
                    if prev is not None:
                        t.wait_ge(sm["dv"].h, prev)
                for kk in range(4):
                    k = rnd * 4 + kk
                    mm = t.transpose(ps_sm[0][:, kk*128:(kk+1)*128],
                                     ysc[:, k*128:(k+1)*128], sb["identf"][:])
                sm["pe"].inc(mm)
            b.tensor(t_t4); pe_t4 = sm["pe"].n
            def v_t4(v, rnd=rnd, cl=cl, pe_t4=pe_t4):
                v.wait_ge(sm["pe"].h, pe_t4)
                src = ps_sm[0][:, 0:512].rearrange("p (a l) -> p a l", l=128)
                dst = y4T.ap().rearrange("p (a l) -> p a l", l=512)[:, rnd*4:(rnd+1)*4, 128*cl:128*cl+128]
                iv = v.tensor_copy(dst, src)
                sm["dv"].inc(iv)
            b.vector(v_t4)
            war["ps_t4"] = sm["dv"].n
            war[("bk", 0)] = sm["dv"].n
        war["ps_y"] = sm["dv"].n
        war[("bufs", buf)] = (pe_y_mms, sm["dv"].n, sm["act"].n)
        c.Hcur, c.Hnxt = c.Hnxt, c.Hcur
    c.tap = dict(ysc=ysc, y4T=y4T, zsb=zsb, hbb=hbb, xdt=xdt, msb0=msb[0], cxp0=cxp[0], xDb=xDb)
    if "outproj" in skip:
        def v_fake(v):
            w(v, sm["pe"]); w(v, sm["dv"])
            iv = v.memset(c.sb["xT"][:, 0:16], 0.0)
            sm["dv"].inc(iv)
        b.vector(v_fake)
        return
    dv_y4 = sm["dv"].n
    wdram = c.dr["w_out"].rearrange("(k p) m -> p k m", p=128)
    ps_o = [ps_y[:, 0:512], ps_y[:, 512:1024]]
    xstage = [esh.enter_context(nc.sbuf_tensor(f"xstg{h}{i}", [128, 1024], F32)) for i in range(2)]
    ldw_t = {}
    pe_after = {}
    groups = [(256 * g, 256) for g in range(8)]
    wsl2 = c.wsl2
    ldw_t[0] = _wslab_dma(c, 0, wsl2[0], wdram[:, :, 0:256], wait_pe=c.wsl_free_pe)
    for gi, (m0, gw) in enumerate(groups):
        if gi + 1 < len(groups):
            m1, gw1 = groups[gi + 1]
            ldw_t[gi+1] = _wslab_dma(c, gi+1, wsl2[(gi+1) % 2], wdram[:, :, m1:m1+gw1],
                                     wait_pe=pe_after.get(gi - 1))
        slab = wsl2[gi % 2]
        def g_xs(g, gi=gi, h=h):
            prev = war.get(("xstage", gi % 2))
            if prev is not None:
                g.wait_ge(sm["dv"].h, prev)
            g.wait_ge(sm["ldm"].h, sm["ldm"].n)
            ii = g.dma_start(out=xstage[gi % 2].ap().rearrange("p (t l) -> p t l", l=512),
                             in_=c.dr["x_fm"].rearrange("t p l -> p t l")[:, gi*2:gi*2+2, 3+512*h:3+512*h+512])
            sm["ldm"].inc(ii, 16)
        b.gpsimd(g_xs); ldm_xs = sm["ldm"].n
        for mt in range(2):
            p = ps_o[mt]
            def t_o(t, gi=gi, mt=mt, p=p, slab=slab, gw=gw):
                t.wait_ge(sm["ldw"].h, ldw_t[gi])
                t.wait_ge(sm["dv"].h, dv_y4)
                prev = war.get(("ps_o", mt))
                if prev is not None:
                    t.wait_ge(sm["dv"].h, prev)
                for k in range(KD):
                    mm = t.matmul(p, slab[:, k*gw + mt*128 : k*gw + mt*128 + 128],
                                  y4T[:, k*512 : (k+1)*512],
                                  start=(k == 0), stop=(k == KD - 1))
                sm["pe"].inc(mm)
            b.tensor(t_o); pe_o = sm["pe"].n
            pe_after[gi] = pe_o
            def v_o(v, gi=gi, mt=mt, p=p, h=h, pe_o=pe_o):
                v.wait_ge(sm["pe"].h, pe_o)
                v.wait_ge(sm["ldm"].h, ldm_xs)
                tile = gi * 2 + mt
                dst = c.sb["xT"][:, tile*TOWN + 512*h : tile*TOWN + 512*h + 512]
                iv = v.scalar_tensor_tensor(out=dst, in0=p, scalar=sb["bout_cols"][:, tile:tile+1],
                                            in1=xstage[gi % 2][:, mt*512:(mt+1)*512],
                                            op0=OP.add, op1=OP.add)
                sm["dv"].inc(iv)
            b.vector(v_o)
            war[("ps_o", mt)] = sm["dv"].n
            war[("xstage", gi % 2)] = sm["dv"].n
    c.wsl_free_pe = max(pe_after.values())


def emit_mamba(c, stop_after=None):
    c.keep = getattr(c, "keep", [])
    """Sublayer 1 complete: LN1, z-spill, per-half in_proj+conv+prep+light,
    state AllGather+combine, main pass per half (+ out_proj/residual)."""
    nc, b, sb, sm = c.nc, c.block, c.sb, c.sm
    esm = ExitStack(); c.keep.append(esm)
    c.S_all = esm.enter_context(nc.sbuf_tensor("S_all", [32, TOWN], F32))
    c.dt_tok = esm.enter_context(nc.sbuf_tensor("dt_tok", [128, NCH * 32], F32))
    c.eend_row = esm.enter_context(nc.sbuf_tensor("eend_row", [1, NCH * 32], F32))
    c.eend_bc = esm.enter_context(nc.sbuf_tensor("eend_bc", [32, NCH * 32], F32))
    c.etot_row = esm.enter_context(nc.sbuf_tensor("etot_row", [1, 32], F32))
    c.Hloc = esm.enter_context(nc.sbuf_tensor("Hloc", [32, DSSM], F32))
    def _tap(idx):
        if "tapH" not in c.dbg:
            return
        def g_tap(g, idx=idx):
            for s2 in ("pe", "act", "dv", "spl", "ldm", "ldw"):
                w(g, sm[s2])
            g.wait_ge(sm["sts"].h, sm["sts"].n)
            ii = g.dma_start(out=c.dbg["tapH"][idx], in_=c.Hloc[0:32, :])
            sm["sts"].inc(ii, 16)
            if idx == 0:
                g.wait_ge(sm["sts"].h, sm["sts"].n)
                ii = g.dma_start(out=c.dbg["tapH"][3, 0:32, 0:256], in_=c.eend_bc[0:32, :])
                sm["sts"].inc(ii, 16)
                g.wait_ge(sm["sts"].h, sm["sts"].n)
                ii = g.dma_start(out=c.dbg["tapH"][3, 0:1, 256:512], in_=c.eend_row[0:1, :])
                sm["sts"].inc(ii, 16)
                g.wait_ge(sm["sts"].h, sm["sts"].n)
                ii = g.dma_start(out=c.dbg["tapH"][3, 0:32, 512:768], in_=c.expmat[0:32, :])
                sm["sts"].inc(ii, 16)
        b.gpsimd(g_tap)
    c._tap = _tap
    es_dtmp = ExitStack(); c.keep.append(es_dtmp)
    c.dt_r = es_dtmp.enter_context(nc.sbuf_tensor("dt_r", [32, 512], F32))
    c.dt_a = es_dtmp.enter_context(nc.sbuf_tensor("dt_a", [32, 512], F32))
    c.dt2 = es_dtmp.enter_context(nc.sbuf_tensor("dt2", [32, TOWN], F32))
    c.dta = es_dtmp.enter_context(nc.sbuf_tensor("dta", [32, TOWN], F32))
    c.expmat = es_dtmp.enter_context(nc.sbuf_tensor("expmat", [128, NCH * 32], F32))
    def g_init(g):
        g.memset(c.Hloc[:], 0.0)
        mg = g.memset(c.etot_row[:], 1.0)
        sm["pool"].inc(mg, 1)
    b.gpsimd(g_init)
    es_lnb = ExitStack(); c.keep.append(es_lnb)
    lnb = es_lnb.enter_context(nc.sbuf_tensor("lnb", [128, KD * TT], BF16))
    c.lnb = lnb
    es_ln = ExitStack(); c.keep.append(es_ln)
    emit_ln(c, 0, TT, lnb, es_ln, deps=[(sm["ldm"].h, sm["ldm"].n)], src_dram=c.dr["x_fm"])
    emit_barrier(c)
    es_ln.close()
    es_wsl = ExitStack(); c.keep.append(es_wsl)
    c.wsl = [es_wsl.enter_context(nc.sbuf_tensor(f"ipw{i}", [128, KD * 512], BF16)) for i in range(2)]
    es_z = ExitStack(); c.keep.append(es_z)
    emit_zspill(c, es_z)
    emit_barrier(c)
    es_z.close()
    c.wsl_free_pe = sm["pe"].n
    for h in range(2):
        es_h = ExitStack(); c.keep.append(es_h)
        xBCraw = es_h.enter_context(nc.sbuf_tensor(f"xBCraw{h}", [128, 20 * 515], F32))
        es_ip = ExitStack(); c.keep.append(es_ip)
        ldw_next = {"pe_prev": c.wsl_free_pe}
        emit_inproj_fmajor_half(c, h, xBCraw, es_ip, c.wsl, ldw_next)
        c.wsl_free_pe = ldw_next["pe_prev"]
        emit_barrier(c)
        es_ip.close()
        emit_conv_prep_light(c, h, xBCraw, es_h)
        emit_barrier(c)
        es_h.close()
        if stop_after == "light" and h == 0:
            return
    c._tap(0)
    es_wsl.close()
    es_lnb.close()
    es_dtmp.close()
    c.Hcur = c.Hloc
    c.Hnxt = esm.enter_context(nc.sbuf_tensor("Hnxt", [32, DSSM], F32))
    c.DbcTok = esm.enter_context(nc.sbuf_tensor("DbcTok", [128, DSSM], F32))
    c.normwB = esm.enter_context(nc.sbuf_tensor("normwB", [128, DSSM], F32))
    es_bc = ExitStack(); c.keep.append(es_bc)
    ps_b = es_bc.enter_context(nc.psum_tensor("ps_dn", [128, 512], F32))
    rowb = es_bc.enter_context(nc.sbuf_tensor("row_dn", [1, D], F32))
    emit_barrier(c)
    pe_rows_prev = None
    for name, dstb in (("d_rep", c.DbcTok), ("normw_row", c.normwB)):
        def g_r(g, name=name, war_pe=pe_rows_prev):
            if war_pe is not None:
                g.wait_ge(sm["pe"].h, war_pe)
            g.wait_ge(sm["ldm"].h, sm["ldm"].n)
            ii = g.dma_start(out=rowb[:], in_=c.dr[name][:])
            sm["ldm"].inc(ii, 16)
        b.gpsimd(g_r); ldm_r = sm["ldm"].n
        for j in range(4):
            def t_r(t, j=j, ldm_r=ldm_r):
                t.wait_ge(sm["ldm"].h, ldm_r)
                w(t, sm["dv"])
                mm = t.matmul(ps_b[:, 0:512], sb["ones1f"][:], rowb[0:1, j*512:(j+1)*512],
                              start=True, stop=True)
                sm["pe"].inc(mm)
            b.tensor(t_r); pe_r = sm["pe"].n
            def v_r(v, j=j, dstb=dstb, pe_r=pe_r):
                v.wait_ge(sm["pe"].h, pe_r)
                iv = v.tensor_copy(dstb[:, j*512:(j+1)*512], ps_b[:, 0:512])
                sm["dv"].inc(iv)
            b.vector(v_r)
        pe_rows_prev = sm["pe"].n
    emit_barrier(c)
    es_bc.close()
    c.sb["xT"] = c.es.enter_context(nc.sbuf_tensor("xT", [128, KD * TOWN], F32, side="right"))
    if stop_after == "prelight":
        return
    if "noxchg" not in getattr(c, "dbg_skip", set()):
        es_x = ExitStack(); c.keep.append(es_x)
        emit_state_exchange(c, es_x)
        emit_barrier(c)
        es_x.close()
    if stop_after == "exchange":
        return
    c._tap(2)
    if stop_after == "taps":
        return
    for h in range(2):
        es_h = ExitStack(); c.keep.append(es_h)
        emit_mamba_main_half(c, h, es_h)
        emit_barrier(c)
        es_h.close()
        if stop_after == "main0" and h == 0:
            return
    emit_barrier(c)
    esm.close()


def emit_moe(c, call_idx):
    nc, b, sb, dr, sm = c.nc, c.block, c.sb, c.dr, c.sm
    esp = ExitStack(); c.keep.append(esp)
    tag = f"m{call_idx}"
    lnb = esp.enter_context(nc.sbuf_tensor(f"lnb{tag}", [128, KD * TOWN], BF16))
    row_mu = esp.enter_context(nc.sbuf_tensor(f"mu{tag}", [1, TOWN], F32))
    row_rstd = esp.enter_context(nc.sbuf_tensor(f"rs{tag}", [1, TOWN], F32))
    es_ln = ExitStack(); c.keep.append(es_ln)
    emit_ln(c, 0, TOWN, lnb, es_ln, deps=[(sm["dv"].h, sm["dv"].n)],
            rows=(row_mu, row_rstd, 0))
    emit_barrier(c)
    es_ln.close()
    wgate = esp.enter_context(nc.sbuf_tensor(f"wg32{tag}", [128, KD * NE], F32))
    eselF = esp.enter_context(nc.sbuf_tensor(f"esel{tag}", [16, 16 * 128], F32))
    biasr = esp.enter_context(nc.sbuf_tensor(f"ebias{tag}", [128, NE], F32))
    cwT = esp.enter_context(nc.sbuf_tensor(f"cwT{tag}", [16, TOWN], F32))
    cs_sb = esp.enter_context(nc.sbuf_tensor(f"cs{tag}", [1, NE], F32))
    otmp = esp.enter_context(nc.sbuf_tensor(f"otmp{tag}", [128, NE], F32))
    gsb = esp.enter_context(nc.sbuf_tensor(f"g{tag}", [128, NE], F32))
    g2sb = esp.enter_context(nc.sbuf_tensor(f"g2{tag}", [128, NE], F32))
    rows = esp.enter_context(nc.sbuf_tensor(f"rows{tag}", [128, 8], F32))
    cw_sb = esp.enter_context(nc.sbuf_tensor(f"cw{tag}", [128, NE], F32))
    es_g = ExitStack(); c.keep.append(es_g)
    ps_g = es_g.enter_context(nc.psum_tensor(f"psg{tag}", [128, 512], F32))
    GRAW, GOUT = slice(0, 16), slice(64, 80)
    war_g = {}
    def g_ld(g):
        for (dst, name, reshape) in ((wgate, "w_gate", True), (eselF, "esel", False), (biasr, "ebias_rep", False)):
            g.wait_ge(sm["ldm"].h, sm["ldm"].n)
            if reshape:
                ii = g.dma_start(out=dst[:, 0:KD*NE].rearrange("p (k e) -> p k e", e=NE),
                                 in_=dr[name].rearrange("(k p) e -> p k e", p=128))
            else:
                ii = g.dma_start(out=dst[:], in_=dr[name][:])
            sm["ldm"].inc(ii, 16)
    b.gpsimd(g_ld); ldm_g = sm["ldm"].n
    def t_cs(t):
        t.wait_ge(sm["ldm"].h, ldm_g)
        for k in range(KD):
            mm = t.matmul(ps_g[0:1, GRAW], sb["onescolf"][:, 0:1], wgate[:, k*NE:(k+1)*NE],
                          start=(k == 0), stop=(k == KD - 1))
        sm["pe"].inc(mm)
    b.tensor(t_cs); pe_cs = sm["pe"].n
    def v_cs(v):
        v.wait_ge(sm["pe"].h, pe_cs)
        iv = v.tensor_copy(cs_sb[0:1, :], ps_g[0:1, GRAW])
        sm["dv"].inc(iv)
    b.vector(v_cs)
    war_g["bank"] = sm["dv"].n
    for tt in range(8):
        def t_gt(t, tt=tt):
            t.wait_ge(sm["dv"].h, war_g["bank"])
            w(t, sm["dv"])
            for k in range(KD):
                mm = t.matmul(ps_g[:, GRAW], sb["xT"][:, k*TOWN + tt*128 : k*TOWN + tt*128 + 128],
                              wgate[:, k*NE:(k+1)*NE], start=(k == 0), stop=(k == KD - 1))
            sm["pe"].inc(mm)
            mm = t.matmul(ps_g[:, GOUT], row_mu[0:1, tt*128:(tt+1)*128], cs_sb[0:1, :],
                          start=True, stop=True)
            sm["pe"].inc(mm)
            mm = t.transpose(ps_g[:, 33:34], row_rstd[0:1, tt*128:(tt+1)*128], sb["identf"][0:1, 0:1])
            sm["pe"].inc(mm)
        b.tensor(t_gt); pe_gt = sm["pe"].n
        def v_gt(v, tt=tt, pe_gt=pe_gt):
            v.wait_ge(sm["pe"].h, pe_gt)
            prev = war_g.get("rows_act")
            if prev is not None:
                v.wait_ge(sm["act"].h, prev)
            iv = v.tensor_copy(rows[:, 7:8], ps_g[:, 33:34])
            sm["dv"].inc(iv); v.wait_ge(sm["dv"].h, sm["dv"].n)
            iv = v.tensor_copy(otmp[:], ps_g[:, GOUT])
            sm["dv"].inc(iv); v.wait_ge(sm["dv"].h, sm["dv"].n)
            iv = v.tensor_sub(gsb[:], ps_g[:, GRAW], otmp[:])
            sm["dv"].inc(iv); v.wait_ge(sm["dv"].h, sm["dv"].n)
            iv = v.tensor_scalar(out=gsb[:], in0=gsb[:], scalar1=rows[:, 7:8], scalar2=None,
                                 op0=OP.mult)
            sm["dv"].inc(iv); v.wait_ge(sm["dv"].h, sm["dv"].n)
            iv = v.reduce_max(out=rows[:, 0:1], in_=gsb[:], axis=AX.X, negate=True)
            sm["dv"].inc(iv)
        b.vector(v_gt); dv_gt = sm["dv"].n
        def s_gt(s, tt=tt, dv_gt=dv_gt):
            s.wait_ge(sm["dv"].h, dv_gt)
            ia = s.activation(out=gsb[:], in_=gsb[:], func=AF.Exp, bias=rows[:, 0:1],
                              accum_out=rows[:, 1:2])
            sm["act"].inc(ia)
        b.scalar(s_gt); act_gt = sm["act"].n
        war_g["rows_act"] = act_gt
        def v_g2(v, tt=tt, act_gt=act_gt):
            v.wait_ge(sm["act"].h, act_gt)
            steps = []
            iv = v.reciprocal(rows[:, 2:3], rows[:, 1:2])
            sm["dv"].inc(iv); v.wait_ge(sm["dv"].h, sm["dv"].n)
            iv = v.tensor_scalar(out=gsb[:], in0=gsb[:], scalar1=rows[:, 2:3], scalar2=None, op0=OP.mult)
            sm["dv"].inc(iv); v.wait_ge(sm["dv"].h, sm["dv"].n)
            iv = v.tensor_add(gsb[:], gsb[:], biasr[:])
            sm["dv"].inc(iv); v.wait_ge(sm["dv"].h, sm["dv"].n)
            iv = v.reduce_max(out=rows[:, 3:4], in_=gsb[:], axis=AX.X)
            sm["dv"].inc(iv); v.wait_ge(sm["dv"].h, sm["dv"].n)
            iv = v.tensor_scalar(out=g2sb[:], in0=gsb[:], scalar1=rows[:, 3:4], scalar2=None, op0=OP.is_ge)
            sm["dv"].inc(iv); v.wait_ge(sm["dv"].h, sm["dv"].n)
            iv = v.scalar_tensor_tensor(out=g2sb[:], in0=g2sb[:], scalar=-1e30, in1=gsb[:],
                                        op0=OP.mult, op1=OP.add)
            sm["dv"].inc(iv); v.wait_ge(sm["dv"].h, sm["dv"].n)
            iv = v.reduce_max(out=rows[:, 4:5], in_=g2sb[:], axis=AX.X)
            sm["dv"].inc(iv); v.wait_ge(sm["dv"].h, sm["dv"].n)
            iv = v.tensor_scalar(out=g2sb[:], in0=gsb[:], scalar1=rows[:, 4:5], scalar2=None, op0=OP.is_ge)
            sm["dv"].inc(iv); v.wait_ge(sm["dv"].h, sm["dv"].n)
            iv = v.tensor_mul(g2sb[:], g2sb[:], gsb[:])
            sm["dv"].inc(iv); v.wait_ge(sm["dv"].h, sm["dv"].n)
            iv = v.tensor_add(rows[:, 5:6], rows[:, 3:4], rows[:, 4:5])
            sm["dv"].inc(iv); v.wait_ge(sm["dv"].h, sm["dv"].n)
            iv = v.tensor_scalar(out=rows[:, 5:6], in0=rows[:, 5:6], scalar1=1e-20, scalar2=None, op0=OP.add)
            sm["dv"].inc(iv); v.wait_ge(sm["dv"].h, sm["dv"].n)
            iv = v.reciprocal(rows[:, 6:7], rows[:, 5:6])
            sm["dv"].inc(iv); v.wait_ge(sm["dv"].h, sm["dv"].n)
            iv = v.tensor_scalar(out=cw_sb[:], in0=g2sb[:], scalar1=rows[:, 6:7], scalar2=None, op0=OP.mult)
            sm["dv"].inc(iv)
        b.vector(v_g2); dv_cw = sm["dv"].n
        def t_ct(t, tt=tt, dv_cw=dv_cw):
            t.wait_ge(sm["dv"].h, dv_cw)
            mm = t.transpose(ps_g[0:16, 128:256], cw_sb[:, 0:16], sb["identf"][:])
            sm["pe"].inc(mm)
        b.tensor(t_ct); pe_ct = sm["pe"].n
        def v_ct(v, tt=tt, pe_ct=pe_ct):
            v.wait_ge(sm["pe"].h, pe_ct)
            iv = v.tensor_copy(cwT[0:16, tt*128:(tt+1)*128], ps_g[0:16, 128:256])
            sm["dv"].inc(iv)
        b.vector(v_ct)
        war_g["tr"] = sm["dv"].n
        war_g["bank"] = sm["dv"].n
    emit_barrier(c)
    es_g.close()
    es_e = ExitStack(); c.keep.append(es_e)
    hsb = es_e.enter_context(nc.sbuf_tensor(f"h{tag}", [128, 8 * TOWN], BF16))
    sgb = es_e.enter_context(nc.sbuf_tensor(f"sg{tag}", [128, TOWN], BF16))
    sub = es_e.enter_context(nc.sbuf_tensor(f"su{tag}", [128, TOWN], BF16))
    cwB = [es_e.enter_context(nc.sbuf_tensor(f"cwB{tag}{i}", [128, 512], F32)) for i in range(2)]
    wguA = [es_e.enter_context(nc.sbuf_tensor(f"wgu{tag}{i}", [128, KD * 256], BF16)) for i in range(4)]
    wdq = [es_e.enter_context(nc.sbuf_tensor(f"wd{tag}{i}", [128, 8 * 256], BF16)) for i in range(2)]
    ps_zg = es_e.enter_context(nc.psum_tensor(f"pszg{tag}", [128, TOWN], F32))
    ps_zu = es_e.enter_context(nc.psum_tensor(f"pszu{tag}", [128, TOWN], F32))
    ps_wd = [es_e.enter_context(nc.psum_tensor(f"pswd{tag}{i}", [128, 512], F32)) for i in range(2)]
    ps_cw = es_e.enter_context(nc.psum_tensor(f"pscw{tag}", [128, 512], F32))
    war = {}
    ldw_q = {}

    def emit_cwb(e):
        for half in range(2):
            def t_cw(t, e=e, half=half):
                prev = war.get("pscw")
                if prev is not None:
                    t.wait_ge(sm["dv"].h, prev)
                t.wait_ge(sm["dv"].h, war_g["tr"])
                mm = t.matmul(ps_cw[:], eselF[:, e*128:(e+1)*128], cwT[0:16, half*512:(half+1)*512],
                              start=True, stop=True)
                sm["pe"].inc(mm)
            b.tensor(t_cw); pe_cw = sm["pe"].n
            def v_cw(v, half=half, pe_cw=pe_cw):
                v.wait_ge(sm["pe"].h, pe_cw)
                prev = war.get("cwB_cons")
                if prev is not None:
                    v.wait_ge(sm["dv"].h, prev)
                iv = v.tensor_copy(cwB[half][:], ps_cw[:])
                sm["dv"].inc(iv)
            b.vector(v_cw)
            war["pscw"] = sm["dv"].n
        war["cwB"] = sm["dv"].n

    def load_wgu(e, q, buf):
        def g_w(g, e=e, q=q, buf=buf):
            prev = war.get(("wgu", buf))
            if prev is not None:
                g.wait_ge(sm["pe"].h, prev)
            for (dst, name) in ((wguA[buf], "w_g"), (wguA[buf + 2], "w_u")):
                g.wait_ge(sm["ldw"].h, sm["ldw"].n)
                ii = g.dma_start(out=dst[:, 0:KD*256].rearrange("p (k f) -> p k f", f=256),
                                 in_=dr[name][e].rearrange("(k p) f -> p k f", p=128)[:, :, q*256:(q+1)*256])
                sm["ldw"].inc(ii, 16)
        b.gpsimd(g_w)
        ldw_q[("gu", e, q)] = sm["ldw"].n

    def load_wd(e, mb, buf):
        def g_w(g, e=e, mb=mb, buf=buf):
            prev = war.get(("wd", buf))
            if prev is not None:
                g.wait_ge(sm["pe"].h, prev)
            g.wait_ge(sm["ldw"].h, sm["ldw"].n)
            ii = g.dma_start(out=wdq[buf][:, 0:8*256].rearrange("p (k m) -> p k m", m=256),
                             in_=dr["w_d"][e].rearrange("(k p) m -> p k m", p=128)[:, :, mb*256:(mb+1)*256])
            sm["ldw"].inc(ii, 16)
        b.gpsimd(g_w)
        ldw_q[("d", e, mb)] = sm["ldw"].n

    load_wgu(0, 0, 0)
    for e in range(NE):
        emit_cwb(e)
        for q in range(4):
            buf = q % 2
            if (e, q) != (NE - 1, 3):
                ne, nq = (e, q + 1) if q < 3 else (e + 1, 0)
                load_wgu(ne, nq, 1 - buf)
            for f in range(2):
                F = q * 2 + f
                def t_gu(t, e=e, q=q, f=f, buf=buf):
                    t.wait_ge(sm["ldw"].h, ldw_q[("gu", e, q)])
                    prev = war.get("zg")
                    if prev is not None:
                        t.wait_ge(sm["act"].h, prev)
                    for k in range(KD):
                        for ch in range(2):
                            mm = t.matmul(ps_zg[:, ch*512:(ch+1)*512],
                                          wguA[buf][:, k*256 + f*128 : k*256 + f*128 + 128],
                                          lnb[:, k*TOWN + ch*512 : k*TOWN + ch*512 + 512],
                                          start=(k == 0), stop=(k == KD - 1))
                    sm["pe"].inc(mm)
                    prev = war.get("zu")
                    if prev is not None:
                        t.wait_ge(sm["dv"].h, prev)
                    for k in range(KD):
                        for ch in range(2):
                            mm = t.matmul(ps_zu[:, ch*512:(ch+1)*512],
                                          wguA[buf + 2][:, k*256 + f*128 : k*256 + f*128 + 128],
                                          lnb[:, k*TOWN + ch*512 : k*TOWN + ch*512 + 512],
                                          start=(k == 0), stop=(k == KD - 1))
                    sm["pe"].inc(mm)
                b.tensor(t_gu); pe_gu = sm["pe"].n
                war[("wgu", buf)] = pe_gu
                def s_sg(s, pe_gu=pe_gu):
                    s.wait_ge(sm["pe"].h, pe_gu)
                    prev = war.get("sg_cons")
                    if prev is not None:
                        s.wait_ge(sm["dv"].h, prev)
                    ia = s.activation(out=sgb[:], in_=ps_zg[:], func=AF.Silu)
                    sm["act"].inc(ia)
                b.scalar(s_sg); act_sg = sm["act"].n
                war["zg"] = act_sg
                def v_su(v, pe_gu=pe_gu):
                    v.wait_ge(sm["pe"].h, pe_gu)
                    v.wait_ge(sm["dv"].h, war["cwB"])
                    iv = v.tensor_tensor(out=sub[:, 0:512], in0=ps_zu[:, 0:512], in1=cwB[0][:], op=OP.mult)
                    sm["dv"].inc(iv); v.wait_ge(sm["dv"].h, sm["dv"].n)
                    iv = v.tensor_tensor(out=sub[:, 512:1024], in0=ps_zu[:, 512:1024], in1=cwB[1][:], op=OP.mult)
                    sm["dv"].inc(iv)
                b.vector(v_su); dv_su = sm["dv"].n
                war["zu"] = dv_su
                war["cwB_cons"] = dv_su
                def v_h(v, F=F, dv_su=dv_su, act_sg=act_sg):
                    v.wait_ge(sm["dv"].h, dv_su)
                    v.wait_ge(sm["act"].h, act_sg)
                    prev = war.get("h_cons")
                    if prev is not None:
                        v.wait_ge(sm["pe"].h, prev)
                    iv = v.tensor_mul(hsb[:, F*TOWN:(F+1)*TOWN], sgb[:], sub[:])
                    sm["dv"].inc(iv)
                b.vector(v_h)
                war["sg_cons"] = sm["dv"].n
        dv_h_all = sm["dv"].n
        load_wd(e, 0, 0)
        for mb in range(8):
            buf = mb % 2
            if mb < 7:
                load_wd(e, mb + 1, 1 - buf)
            for m2 in range(2):
                for ch in range(2):
                    pw = ps_wd[(m2 * 2 + ch) % 2]
                    def t_wd(t, e=e, mb=mb, m2=m2, ch=ch, buf=buf, pw=pw, dv_h_all=dv_h_all):
                        t.wait_ge(sm["ldw"].h, ldw_q[("d", e, mb)])
                        t.wait_ge(sm["dv"].h, dv_h_all)
                        prev = war.get(("pswd", (m2 * 2 + ch) % 2))
                        if prev is not None:
                            t.wait_ge(sm["dv"].h, prev)
                        for k in range(8):
                            mm = t.matmul(pw[:], wdq[buf][:, k*256 + m2*128 : k*256 + m2*128 + 128],
                                          hsb[:, k*TOWN + ch*512 : k*TOWN + ch*512 + 512],
                                          start=(k == 0), stop=(k == 7))
                        sm["pe"].inc(mm)
                    b.tensor(t_wd); pe_wd = sm["pe"].n
                    war[("wd", buf)] = pe_wd
                    war["h_cons"] = pe_wd
                    def v_wd(v, mb=mb, m2=m2, ch=ch, pw=pw, pe_wd=pe_wd):
                        v.wait_ge(sm["pe"].h, pe_wd)
                        tile = mb * 2 + m2
                        dst = sb["xT"][:, tile*TOWN + ch*512 : tile*TOWN + ch*512 + 512]
                        iv = v.tensor_tensor(out=dst, in0=dst, in1=pw[:], op=OP.add)
                        sm["dv"].inc(iv)
                    b.vector(v_wd)
                    war[("pswd", (m2 * 2 + ch) % 2)] = sm["dv"].n
    emit_barrier(c)
    es_e.close()
    es_g2 = None
    esp.close()


def emit_attn(c):
    nc, b, sb, dr, sm = c.nc, c.block, c.sb, c.dr, c.sm
    esa = ExitStack(); c.keep.append(esa)
    qT = esa.enter_context(nc.sbuf_tensor("qT", [128, HQ * TOWN], BF16))
    kT = esa.enter_context(nc.sbuf_tensor("kT", [128, NKV * 1536], BF16))
    vsb = esa.enter_context(nc.sbuf_tensor("vsb", [128, 12 * 512], BF16))
    es_p = ExitStack(); c.keep.append(es_p)
    lnb = es_p.enter_context(nc.sbuf_tensor("lnb_a", [128, KD * TOWN], BF16))
    es_ln = ExitStack(); c.keep.append(es_ln)
    emit_ln(c, 0, TOWN, lnb, es_ln, deps=[(sm["dv"].h, sm["dv"].n)])
    emit_barrier(c)
    es_ln.close()
    wsl = [es_p.enter_context(nc.sbuf_tensor(f"aw{i}", [128, KD * 256], BF16)) for i in range(2)]
    ps_p = [es_p.enter_context(nc.psum_tensor(f"ps_ap{i}", [128, 512], F32)) for i in range(2)]
    war = {}
    ldw_t = {}
    pe_after = {}

    def wslab(idx, dram3d, wait_pe=None):
        def g_w(g, idx=idx, dram3d=dram3d, wait_pe=wait_pe):
            if wait_pe is not None:
                g.wait_ge(sm["pe"].h, wait_pe)
            g.wait_ge(sm["ldw"].h, sm["ldw"].n)
            ii = g.dma_start(out=wsl[idx % 2][:, 0:KD*256].rearrange("p (k m) -> p k m", m=256),
                             in_=dram3d)
            sm["ldw"].inc(ii, 16)
        b.gpsimd(g_w)
        ldw_t[idx] = sm["ldw"].n

    wq3 = dr["wq_t"].rearrange("(k p) m -> p k m", p=128)
    wk3 = dr["wk_t"].rearrange("(k p) m -> p k m", p=128)
    wv3 = dr["wv_t"].rearrange("(k p) m -> p k m", p=128)
    plan = [("q", g) for g in range(8)] + [("k", 0), ("k", 1), ("v", 0), ("v", 1)]
    def dram_for(item):
        kind, g = item
        if kind == "q":
            return wq3[:, :, g*256:(g+1)*256]
        return (wk3 if kind == "k" else wv3)[:, :, g*256:(g+1)*256]
    wslab(0, dram_for(plan[0]))
    for gi, item in enumerate(plan):
        if gi + 1 < len(plan):
            wslab(gi + 1, dram_for(plan[gi + 1]), wait_pe=pe_after.get(gi - 1))
        kind, grp = item
        slab = wsl[gi % 2]
        if kind in ("q", "k"):
            for mt in range(2):
                for ch in range(2):
                    p = ps_p[mt]
                    def t_p(t, gi=gi, mt=mt, ch=ch, p=p, slab=slab):
                        t.wait_ge(sm["ldw"].h, ldw_t[gi])
                        prev = war.get(("pp", mt))
                        if prev is not None:
                            t.wait_ge(sm["dv"].h, prev)
                        for k in range(KD):
                            mm = t.matmul(p[:], slab[:, k*256 + mt*128 : k*256 + mt*128 + 128],
                                          lnb[:, k*TOWN + ch*512 : k*TOWN + ch*512 + 512],
                                          start=(k == 0), stop=(k == KD - 1))
                        sm["pe"].inc(mm)
                    b.tensor(t_p); pe_p = sm["pe"].n
                    pe_after[gi] = pe_p
                    def v_p(v, kind=kind, grp=grp, mt=mt, ch=ch, p=p, pe_p=pe_p):
                        v.wait_ge(sm["pe"].h, pe_p)
                        hh = grp * 2 + mt
                        if kind == "q":
                            dst = qT[:, hh*TOWN + ch*512 : hh*TOWN + ch*512 + 512]
                        else:
                            dst = kT[:, hh*1536 + 512 + ch*512 : hh*1536 + 512 + ch*512 + 512]
                        iv = v.tensor_copy(dst, p[:])
                        sm["dv"].inc(iv)
                    b.vector(v_p)
                    war[("pp", mt)] = sm["dv"].n
        else:
            for tt in range(8):
                p = ps_p[tt % 2]
                def t_v(t, gi=gi, grp=grp, tt=tt, p=p, slab=slab):
                    t.wait_ge(sm["ldw"].h, ldw_t[gi])
                    prev = war.get(("pp", tt % 2))
                    if prev is not None:
                        t.wait_ge(sm["dv"].h, prev)
                    for k in range(KD):
                        mm = t.matmul(p[:, 0:256], lnb[:, k*TOWN + tt*128 : k*TOWN + tt*128 + 128],
                                      slab[:, k*256:(k+1)*256], start=(k == 0), stop=(k == KD - 1))
                    sm["pe"].inc(mm)
                b.tensor(t_v); pe_v = sm["pe"].n
                pe_after[gi] = pe_v
                def v_v(v, grp=grp, tt=tt, p=p, pe_v=pe_v):
                    v.wait_ge(sm["pe"].h, pe_v)
                    iv = v.tensor_copy(vsb[:, (4 + tt)*512 + grp*256 : (4 + tt)*512 + grp*256 + 256],
                                       p[:, 0:256])
                    sm["dv"].inc(iv)
                b.vector(v_v)
                war[("pp", tt % 2)] = sm["dv"].n
    emit_barrier(c)
    es_p.close()
    es_r = ExitStack(); c.keep.append(es_r)
    tabs = {}
    for nm in ("cos_q", "sin_q", "cos_k", "sin_k"):
        tabs[nm] = es_r.enter_context(nc.sbuf_tensor(f"tab_{nm}", [128, TOWN], BF16))
    qsw = es_r.enter_context(nc.sbuf_tensor("qsw", [128, 8 * TOWN], BF16))
    def g_tabs(g):
        for nm in ("cos_q", "sin_q", "cos_k", "sin_k"):
            g.wait_ge(sm["ldm"].h, sm["ldm"].n)
            ii = g.dma_start(out=tabs[nm][:], in_=dr[nm][:])
            sm["ldm"].inc(ii, 16)
    b.gpsimd(g_tabs); ldm_tabs = sm["ldm"].n
    for rnd in range(2):
        def g_sw(g, rnd=rnd):
            w(g, sm["dv"])
            prev = war.get("qsw")
            if prev is not None:
                g.wait_ge(sm["dv"].h, prev)
            g.wait_ge(sm["spl"].h, sm["spl"].n)
            ii = g.dma_start(out=qsw[0:64, :], in_=qT[64:128, rnd*8*TOWN:(rnd+1)*8*TOWN])
            sm["spl"].inc(ii, 16)
            g.wait_ge(sm["spl"].h, sm["spl"].n)
            ii = g.dma_start(out=qsw[64:128, :], in_=qT[0:64, rnd*8*TOWN:(rnd+1)*8*TOWN])
            sm["spl"].inc(ii, 16)
        b.gpsimd(g_sw); spl_sw = sm["spl"].n
        def v_rope(v, rnd=rnd, spl_sw=spl_sw):
            v.wait_ge(sm["spl"].h, spl_sw)
            v.wait_ge(sm["ldm"].h, ldm_tabs)
            qv = qT.ap().rearrange("p (h l) -> p h l", l=TOWN)[:, rnd*8:(rnd+1)*8, :]
            sv = qsw.ap().rearrange("p (h l) -> p h l", l=TOWN)
            iv = v.tensor_tensor(out=sv, in0=sv,
                                 in1=tabs["sin_q"].ap().unsqueeze(1).broadcast_to([128, 8, TOWN]),
                                 op=OP.mult)
            sm["dv"].inc(iv); v.wait_ge(sm["dv"].h, sm["dv"].n)
            iv = v.tensor_tensor(out=qv, in0=qv,
                                 in1=tabs["cos_q"].ap().unsqueeze(1).broadcast_to([128, 8, TOWN]),
                                 op=OP.mult)
            sm["dv"].inc(iv); v.wait_ge(sm["dv"].h, sm["dv"].n)
            iv = v.tensor_tensor(out=qv, in0=qv, in1=sv, op=OP.add)
            sm["dv"].inc(iv)
        b.vector(v_rope)
        war["qsw"] = sm["dv"].n
    def g_swk(g):
        prev = war.get("qsw")
        if prev is not None:
            g.wait_ge(sm["dv"].h, prev)
        g.wait_ge(sm["spl"].h, sm["spl"].n)
        kv = kT.ap().rearrange("p (h l) -> p h l", l=1536)[:, :, 512:1536]
        ii = g.dma_start(out=qsw[0:64, 0:4*TOWN].rearrange("p (h l) -> p h l", l=TOWN), in_=kv[64:128])
        sm["spl"].inc(ii, 16)
        g.wait_ge(sm["spl"].h, sm["spl"].n)
        ii = g.dma_start(out=qsw[64:128, 0:4*TOWN].rearrange("p (h l) -> p h l", l=TOWN), in_=kv[0:64])
        sm["spl"].inc(ii, 16)
    b.gpsimd(g_swk); spl_swk = sm["spl"].n
    def v_ropek(v, spl_swk=spl_swk):
        v.wait_ge(sm["spl"].h, spl_swk)
        v.wait_ge(sm["ldm"].h, ldm_tabs)
        kv = kT.ap().rearrange("p (h l) -> p h l", l=1536)[:, :, 512:1536]
        sv = qsw.ap().rearrange("p (h l) -> p h l", l=TOWN)[:, 0:4, :]
        iv = v.tensor_tensor(out=sv, in0=sv,
                             in1=tabs["sin_k"].ap().unsqueeze(1).broadcast_to([128, 4, TOWN]), op=OP.mult)
        sm["dv"].inc(iv); v.wait_ge(sm["dv"].h, sm["dv"].n)
        iv = v.tensor_tensor(out=kv, in0=kv,
                             in1=tabs["cos_k"].ap().unsqueeze(1).broadcast_to([128, 4, TOWN]), op=OP.mult)
        sm["dv"].inc(iv); v.wait_ge(sm["dv"].h, sm["dv"].n)
        iv = v.tensor_tensor(out=kv, in0=kv, in1=sv, op=OP.add)
        sm["dv"].inc(iv)
    b.vector(v_ropek); dv_rope = sm["dv"].n
    def g_kv(g):
        g.wait_ge(sm["dv"].h, dv_rope)
        g.wait_ge(sm["spl"].h, sm["spl"].n)
        src_k = kT.ap().rearrange("p (h l) -> p h l", l=1536)[:, :, 1024:1536]
        ii = g.dma_start(out=c.kvag_in.ap()[:, 0:2048].rearrange("p (h l) -> p h l", l=512), in_=src_k)
        sm["spl"].inc(ii, 16)
        g.wait_ge(sm["spl"].h, sm["spl"].n)
        ii = g.dma_start(out=c.kvag_in.ap()[:, 2048:4096], in_=vsb[:, 8*512:12*512])
        sm["spl"].inc(ii, 16)
        g.wait_ge(sm["spl"].h, sm["spl"].n)
        ii = g.collective_compute("AllGather", OP.bypass, replica_groups=[list(range(NC))],
                                  ins=[c.kvag_in.ap().opt()], outs=[c.kvag_out.ap().opt()])
        sm["cc"].inc(ii, 1)
        g.wait_ge(sm["cc"].h, sm["cc"].n)
        off = g.value_load(sb["halo_src"][0:1, 0:1])
        g.wait_ge(sm["ldm"].h, sm["ldm"].n)
        ii = g.dma_start(out=kT.ap().rearrange("p (h l) -> p h l", l=1536)[:, :, 0:512],
                         in_=c.kvag_out.ap()[DynSlice(off, 128), 0:2048].rearrange("p (h l) -> p h l", l=512))
        sm["ldm"].inc(ii, 16)
        g.wait_ge(sm["ldm"].h, sm["ldm"].n)
        ii = g.dma_start(out=vsb[:, 0:4*512], in_=c.kvag_out.ap()[DynSlice(off, 128), 2048:4096])
        sm["ldm"].inc(ii, 16)
    b.gpsimd(g_kv); ldm_kv = sm["ldm"].n
    emit_barrier(c)
    es_r.close()
    es_s = ExitStack(); c.keep.append(es_s)
    yatt = es_s.enter_context(nc.sbuf_tensor("yatt", [128, HQ * TOWN], BF16))
    rden = es_s.enter_context(nc.sbuf_tensor("rden", [1, 512], F32))
    rbb = es_s.enter_context(nc.sbuf_tensor("rbb", [128, 512], F32))
    es_s2 = ExitStack(); c.keep.append(es_s2)
    maskb = es_s2.enter_context(nc.sbuf_tensor("maskb", [128, 2 * 8 * 512], BF16))
    pT = es_s2.enter_context(nc.sbuf_tensor("pT", [128, 8 * 512], BF16))
    ps_s = [es_s2.enter_context(nc.psum_tensor(f"ps_sc{i}", [128, 512], F32)) for i in range(2)]
    ps_d = es_s2.enter_context(nc.psum_tensor("ps_den", [1, 512], F32))
    ps_av = es_s2.enter_context(nc.psum_tensor("ps_avo", [128, 512], F32))
    ps_rb = es_s2.enter_context(nc.psum_tensor("ps_rbc", [128, 512], F32))
    def g_mask(g):
        g.wait_ge(sm["ldm"].h, sm["ldm"].n)
        ii = g.dma_start(out=maskb[:, 0:8192].rearrange("p (a b l) -> p a b l", b=8, l=512),
                         in_=dr["att_mask"].rearrange("a b p l -> p a b l"))
        sm["ldm"].inc(ii, 16)
    b.gpsimd(g_mask); ldm_msk = sm["ldm"].n
    for hh in range(HQ):
        for qc in range(2):
            for kt in range(8):
                p = ps_s[kt % 2]
                def t_s(t, hh=hh, qc=qc, kt=kt, p=p):
                    t.wait_ge(sm["ldm"].h, ldm_msk)
                    t.wait_ge(sm["ldm"].h, ldm_kv)
                    prev = war.get(("ps_s", kt % 2))
                    if prev is not None:
                        t.wait_ge(sm["act"].h, prev)
                    t.matmul(p[:], sb["identb"][:], maskb[:, (qc*8 + kt)*512 : (qc*8 + kt)*512 + 512],
                             start=True, stop=False)
                    mm = t.matmul(p[:], kT[:, hh//4*1536 + qc*512 + kt*128 : hh//4*1536 + qc*512 + kt*128 + 128],
                                  qT[:, hh*TOWN + qc*512 : hh*TOWN + qc*512 + 512],
                                  start=False, stop=True)
                    sm["pe"].inc(mm)
                b.tensor(t_s); pe_s = sm["pe"].n
                def s_e(s, hh=hh, qc=qc, kt=kt, p=p, pe_s=pe_s):
                    s.wait_ge(sm["pe"].h, pe_s)
                    prev = war.get("pT_cons")
                    if prev is not None:
                        s.wait_ge(sm["pe"].h, prev)
                    ia = s.activation(out=pT[:, kt*512:(kt+1)*512], in_=p[:], func=AF.Exp)
                    sm["act"].inc(ia)
                b.scalar(s_e)
                war[("ps_s", kt % 2)] = sm["act"].n
            act_p = sm["act"].n
            def t_dav(t, hh=hh, qc=qc, act_p=act_p):
                t.wait_ge(sm["act"].h, act_p)
                prev = war.get("ps_d")
                if prev is not None:
                    t.wait_ge(sm["dv"].h, prev)
                for kt in range(8):
                    mm = t.matmul(ps_d[0:1, :], sb["onescolb"][:, 0:1], pT[:, kt*512:(kt+1)*512],
                                  start=(kt == 0), stop=(kt == 7))
                sm["pe"].inc(mm)
                prev = war.get("ps_av")
                if prev is not None:
                    t.wait_ge(sm["dv"].h, prev)
                for kt in range(8):
                    vcol = (qc*4 + kt)*512 + (hh//4)*128
                    mm = t.matmul(ps_av[:], vsb[:, vcol:vcol+128],
                                  pT[:, kt*512:(kt+1)*512], start=(kt == 0), stop=(kt == 7))
                sm["pe"].inc(mm)
            b.tensor(t_dav); pe_dav = sm["pe"].n
            war["pT_cons"] = pe_dav
            def v_r(v, pe_dav=pe_dav):
                v.wait_ge(sm["pe"].h, pe_dav)
                iv = v.reciprocal(rden[0:1, :], ps_d[0:1, :])
                sm["dv"].inc(iv)
            b.vector(v_r); dv_r = sm["dv"].n
            war["ps_d"] = dv_r
            def t_rb(t, dv_r=dv_r):
                t.wait_ge(sm["dv"].h, dv_r)
                prev = war.get("ps_rb")
                if prev is not None:
                    t.wait_ge(sm["dv"].h, prev)
                mm = t.matmul(ps_rb[:], sb["ones1f"][:], rden[0:1, :], start=True, stop=True)
                sm["pe"].inc(mm)
            b.tensor(t_rb); pe_rb = sm["pe"].n
            def v_y(v, hh=hh, qc=qc, pe_rb=pe_rb):
                v.wait_ge(sm["pe"].h, pe_rb)
                iv = v.tensor_copy(rbb[:], ps_rb[:])
                sm["dv"].inc(iv); v.wait_ge(sm["dv"].h, sm["dv"].n)
                iv = v.tensor_tensor(out=yatt[:, hh*TOWN + qc*512 : hh*TOWN + qc*512 + 512],
                                     in0=ps_av[:], in1=rbb[:], op=OP.mult)
                sm["dv"].inc(iv)
            b.vector(v_y)
            war["ps_rb"] = sm["dv"].n
            war["ps_av"] = sm["dv"].n
    emit_barrier(c)
    es_s2.close()
    es_w = ExitStack(); c.keep.append(es_w)
    wsl2 = [es_w.enter_context(nc.sbuf_tensor(f"awo{i}", [128, KD * 256], BF16)) for i in range(2)]
    ps_o = [es_w.enter_context(nc.psum_tensor(f"ps_ao{i}", [128, 512], F32)) for i in range(2)]
    wo3 = dr["wo_t"].rearrange("(k p) m -> p k m", p=128)
    ldw_o = {}
    pe_after_o = {}
    def wslab_o(idx, wait_pe=None):
        def g_w(g, idx=idx, wait_pe=wait_pe):
            if wait_pe is not None:
                g.wait_ge(sm["pe"].h, wait_pe)
            g.wait_ge(sm["ldw"].h, sm["ldw"].n)
            ii = g.dma_start(out=wsl2[idx % 2][:, 0:KD*256].rearrange("p (k m) -> p k m", m=256),
                             in_=wo3[:, :, idx*256:(idx+1)*256])
            sm["ldw"].inc(ii, 16)
        b.gpsimd(g_w)
        ldw_o[idx] = sm["ldw"].n
    wslab_o(0)
    for gi in range(8):
        if gi + 1 < 8:
            wslab_o(gi + 1, wait_pe=pe_after_o.get(gi - 1))
        for mt in range(2):
            for ch in range(2):
                p = ps_o[mt]
                def t_o(t, gi=gi, mt=mt, ch=ch, p=p):
                    t.wait_ge(sm["ldw"].h, ldw_o[gi])
                    w(t, sm["dv"])
                    prev = war.get(("ps_o", mt))
                    if prev is not None:
                        t.wait_ge(sm["dv"].h, prev)
                    for k in range(KD):
                        mm = t.matmul(p[:], wsl2[gi % 2][:, k*256 + mt*128 : k*256 + mt*128 + 128],
                                      yatt[:, k*TOWN + ch*512 : k*TOWN + ch*512 + 512],
                                      start=(k == 0), stop=(k == KD - 1))
                    sm["pe"].inc(mm)
                b.tensor(t_o); pe_o = sm["pe"].n
                pe_after_o[gi] = pe_o
                def v_o(v, gi=gi, mt=mt, ch=ch, p=p, pe_o=pe_o):
                    v.wait_ge(sm["pe"].h, pe_o)
                    tile = gi * 2 + mt
                    dst = sb["xT"][:, tile*TOWN + ch*512 : tile*TOWN + ch*512 + 512]
                    iv = v.tensor_tensor(out=dst, in0=dst, in1=p[:], op=OP.add)
                    sm["dv"].inc(iv)
                b.vector(v_o)
                war[("ps_o", mt)] = sm["dv"].n
    emit_barrier(c)
    es_w.close()
    es_s.close()
    esa.close()


def emit_output(c):
    nc, b, sm = c.nc, c.block, c.sm
    def g_out(g):
        for s in ("dv", "act", "pe", "pool", "spl", "ldm", "ldw", "lds", "cc"):
            if sm[s].n > 0:
                w(g, sm[s])
        g.wait_ge(sm["sts"].h, sm["sts"].n)
        ii = g.dma_start(out=c.out.rearrange("t p l -> p t l"),
                         in_=c.sb["xT"].ap().rearrange("p (t l) -> p t l", l=TOWN))
        sm["sts"].inc(ii, 16)
        g.wait_ge(sm["sts"].h, sm["sts"].n)
    b.gpsimd(g_out)


def kernel(**inputs):
    in_maps, _ = prep_inputs(inputs)
    c = build(debug=None)
    emit_phase0(c)
    emit_barrier(c)
    emit_mamba(c)
    emit_moe(c, 1)
    emit_attn(c)
    emit_moe(c, 2)
    emit_output(c)
    c.keep.append(c.es)
    res = run_bass_kernel_spmd(c.nc, in_maps, core_ids=list(range(NC)))
    parts = [res.results[i]["out"].transpose(2, 0, 1).reshape(TOWN, D) for i in range(NC)]
    return np.ascontiguousarray(np.concatenate(parts, axis=0)[None, :, :]).astype(np.float32)
